# version 06
# baseline (speedup 1.0000x reference)
import contextlib
import math

import numpy as np
import ml_dtypes

import concourse.bass as bass
import concourse.mybir as mybir
from concourse.bass_utils import run_bass_kernel_spmd
from concourse.alu_op_type import AluOpType as OP

F32 = mybir.dt.float32
F32R = mybir.dt.float32r
BF16 = mybir.dt.bfloat16
AF = mybir.ActivationFunctionType
AX = mybir.AxisListType

DEBUG = False
PHASES = "ABCDEF"
LAST_RES = None

NCORE = 8
T = 8192
D = 2048
NH = 16
HD = 128
DC = 16
NT = 64
ML = 8
TL = ML * 128
E = 16
ICN = 11
NSH = 2
EPS = 1e-5
CSH = 16.0
ISQ = 1.0 / math.sqrt(HD)

OFF_IDENT = 0
OFF_ONES = 256
OFF_ONESROW = 288
OFF_ONESROW_F = 544
OFF_MASK = 1056
OFF_STAT = 3104
OFF_COMB = 3360
OFF_GSC = 3872
OFF_WG = 4256
OFF_SCR = 4768
DYN = 8896
OFF_LROW = DYN + 184320
OFF_LROW_FR = DYN + 188416

OFF_WK = DYN
OFF_WV = DYN + 65536
OFF_XRAW = DYN + 131072
OFF_XLNT = DYN + 163840
OFF_XLNF = DYN + 167936
OFF_KVST = DYN + 184320
OFF_WQ = DYN
OFF_XLNOF = DYN + 65536
OFF_QFM = DYN + 98304
OFF_XRAWB = DYN + 131072
OFF_XLNTB = DYN + 163840
OFF_KBUF = DYN
OFF_VTM = DYN + 32768
OFF_PTB = DYN + 147456
OFF_ZFM = DYN + 151552
OFF_WO = DYN
OFF_AOUT = DYN + 65536
OFF_X2T = DYN + 73728
OFF_XRAWD = DYN + 81920
OFF_XTT = DYN + 90112
OFF_XTF = DYN + 98304
OFF_H = DYN
OFF_W1B = DYN + 22528
OFF_W3B = DYN + 30720
OFF_W2A = DYN + 38912
OFF_Y = DYN + 131072
OFF_X2F = DYN + 8192 - 8192
OFF_FOUT = DYN + 8192

SBUF_BASE = 16512
SBUF_END = SBUF_BASE + DYN + 196608
assert SBUF_END <= 229247, SBUF_END


def _sb(nc, name, shape, dtype, off):
    return nc.alloc_sbuf_tensor_at(name, shape, dtype, offset=SBUF_BASE + off)


class Ctx:
    pass


def build_nc():
    nc = bass.Bass("TRN2", target_bir_lowering=False)
    c = Ctx()
    c.nc = nc

    din = lambda n, s, dt=F32: nc.dram_tensor(n, s, dt, kind="ExternalInput").ap()
    c.x_shard = din("x_shard", [TL, D])
    c.x_full = din("x_full", [T, D])
    c.mask8 = din("mask8", [128, 8 * 128], BF16)
    c.ident_in = din("ident_in", [128, 128], BF16)
    c.wq_t = din("wq_t", [128, DC, DC, 128])
    c.wk_t = din("wk_t", [128, DC, DC, 128])
    c.wv_t = din("wv_t", [128, DC, DC, 128])
    c.wo_t = din("wo_t", [DC, 128, D])
    c.wg_t = din("wg_t", [128, DC, E])
    c.w1_t = din("w1_t", [E, ICN, 128, DC, 128])
    c.w3_t = din("w3_t", [E, ICN, 128, DC, 128])
    c.w2_t = din("w2_t", [E, ICN, 128, D])
    c.ws1_t = din("ws1_t", [NSH, ICN, 128, DC, 128])
    c.ws3_t = din("ws3_t", [NSH, ICN, 128, DC, 128])
    c.ws2_t = din("ws2_t", [NSH, ICN, 128, D])
    c.out = nc.dram_tensor("out", [TL, D], F32, kind="ExternalOutput").ap()
    c.k_sc = nc.dram_tensor("k_sc", [D, T], BF16, kind="Internal").ap()
    c.v_sc = nc.dram_tensor("v_sc", [4, T, 4 * HD], BF16, kind="Internal").ap()
    c.x2_sc = nc.dram_tensor("x2_sc", [TL, D], F32, kind="Internal").ap()
    if DEBUG:
        dout = lambda n, s, dt: nc.dram_tensor(n, s, dt, kind="ExternalOutput").ap()
        c.dbg_k = dout("dbg_k", [D, T], BF16)
        c.dbg_v = dout("dbg_v", [4, T, 4 * HD], BF16)
        c.dbg_q = dout("dbg_q", [128, NH * 1024], BF16)
        c.dbg_z = dout("dbg_z", [128, NH * 1024], BF16)
        c.dbg_comb = dout("dbg_comb", [128, 128], F32)
        c.dbg_xtf = dout("dbg_xtf", [128, DC * 1024], BF16)
        c.dbg_x2 = dout("dbg_x2", [TL, D], F32)
        c.dbg_y = dout("dbg_y", [128, ML * 2048], F32)
        c.dbg_gsc = dout("dbg_gsc", [128, 96], F32)
        c.dbg_wk = dout("dbg_wk", [128, 2048], BF16)
        c.dbg_xlnf = dout("dbg_xlnf", [128, DC * 512], BF16)
        c.dbg_stat = dout("dbg_stat", [128, 64], F32)
        c.dbg_xraw = dout("dbg_xraw", [128, 4 * 2048], F32)
        c.dbg_vtm = dout("dbg_vtm", [128, 8192], BF16)
        c.dbg_lrow = dout("dbg_lrow", [1, 1024], F32)
        c.dbg_rscr = dout("dbg_rscr", [128, 1024], F32)

    c.ident = _sb(nc, "ident", [128, 128], BF16, OFF_IDENT)
    c.ones = _sb(nc, "ones", [128, 1], BF16, OFF_ONES)
    c.onesrow = _sb(nc, "onesrow", [1, 128], BF16, OFF_ONESROW)
    c.onesrow_f = _sb(nc, "onesrow_f", [1, 128], F32, OFF_ONESROW_F)
    c.mask = _sb(nc, "mask", [128, 8 * 128], BF16, OFF_MASK)
    c.stat = _sb(nc, "stat", [128, 64], F32, OFF_STAT)
    c.comb = _sb(nc, "comb", [128, 128], F32, OFF_COMB)
    c.gsc = _sb(nc, "gsc", [128, 96], F32, OFF_GSC)
    c.gsc8 = _sb(nc, "gsc8", [128, 8 * 96], F32, 65536 + 8896)
    c.wg = _sb(nc, "wg", [128, DC * E], BF16, OFF_WG)
    c.lrow = _sb(nc, "lrow", [1, 1024], F32, OFF_LROW)
    c.lrow2 = _sb(nc, "lrow2", [1, 1024], F32, OFF_LROW_FR)
    c.rscr = _sb(nc, "rscr", [128, 1024], F32, OFF_SCR)
    c.sqs = _sb(nc, "sqs", [128, 1024], F32, OFF_SCR)
    c.sil = _sb(nc, "sil", [128, 1024], F32, OFF_SCR)

    c.wk = _sb(nc, "wk", [128, DC * DC * 128], BF16, OFF_WK)
    c.wv = _sb(nc, "wv", [128, DC * DC * 128], BF16, OFF_WV)
    c.xraw = _sb(nc, "xraw", [128, 4 * 2048], F32, OFF_XRAW)
    c.xlnt = _sb(nc, "xlnt", [128, 2048], BF16, OFF_XLNT)
    c.xlnf = _sb(nc, "xlnf", [128, DC * 512], BF16, OFF_XLNF)
    c.kvst = _sb(nc, "kvst", [128, 4 * 512], BF16, OFF_KVST)

    c.wq = _sb(nc, "wq", [128, DC * DC * 128], BF16, OFF_WQ)
    c.xlnof = _sb(nc, "xlnof", [128, DC * 1024], BF16, OFF_XLNOF)
    c.qfm = _sb(nc, "qfm", [128, NH * 1024], BF16, OFF_QFM)
    c.xrawb = _sb(nc, "xrawb", [128, 2048], F32, OFF_XRAWB)
    c.xlntb = _sb(nc, "xlntb", [128, 2048], BF16, OFF_XLNTB)

    c.kbuf = _sb(nc, "kbuf", [128, 2 * 8192], BF16, OFF_KBUF)
    c.vtm = _sb(nc, "vtm", [128, 2 * 8192], BF16, OFF_VTM)
    c.ptb = _sb(nc, "ptb", [128, 2 * 1024], BF16, OFF_PTB)
    c.zfm = _sb(nc, "zfm", [128, NH * 1024], BF16, OFF_ZFM)

    c.wo = _sb(nc, "wo", [128, DC * 2048], BF16, OFF_WO)
    c.aout = _sb(nc, "aout", [128, 2048], F32, OFF_AOUT)
    c.x2t = _sb(nc, "x2t", [128, 2048], F32, OFF_X2T)
    c.xrawd = _sb(nc, "xrawd", [128, 2048], F32, OFF_XRAWD)
    c.xtt = _sb(nc, "xtt", [128, 2048], BF16, OFF_XTT)
    c.xtf = _sb(nc, "xtf", [128, DC * 1024], BF16, OFF_XTF)

    c.h = _sb(nc, "h", [128, ICN * 1024], BF16, OFF_H)
    c.w1b = _sb(nc, "w1b", [128, 2 * 2048], BF16, OFF_W1B)
    c.w3b = _sb(nc, "w3b", [128, 2 * 2048], BF16, OFF_W3B)
    c.w2a = _sb(nc, "w2a", [128, ICN * 2048], BF16, OFF_W2A)
    c.y = _sb(nc, "y", [128, ML * 2048], F32, OFF_Y)

    c.x2f = _sb(nc, "x2f", [128, 2048], F32, DYN)
    c.fout = _sb(nc, "fout", [128, 2048], F32, DYN + 8192)

    c.ST_MEAN = 0
    c.ST_VAR = 1
    c.ST_RSTD = 2
    c.ST_LNV = 3
    c.ST_EPS = 4
    c.ST_NEGC = 5
    c.ST_NMAX = 6
    c.ST_SEXP = 7
    c.ST_RSE = 8
    c.ST_MEAN2 = 12
    c.ST_VAR2 = 13

    return c


def build_program(c):
    nc = c.nc
    stack = contextlib.ExitStack()
    with stack:
        sem = lambda n: stack.enter_context(nc.semaphore(n))
        c.sem_sq = sem("sq")
        c.sem_gq = sem("gq")
        c.sem_tm = sem("tm")
        c.sem_vm = sem("vm")
        c.sem_am = sem("am")
        c.sem_ph = sem("ph")
        pbA = stack.enter_context(nc.psum_tensor("pbA", [128, 1024], F32))
        pbB = stack.enter_context(nc.psum_tensor("pbB", [128, 1024], F32))
        pbC = stack.enter_context(nc.psum_tensor("pbC", [128, 1024], F32))
        pbD = stack.enter_context(nc.psum_tensor("pbD", [128, 1024], F32))
        block = stack.enter_context(nc.Block())

        sq = [0]
        gq = [0]
        tm = [0]
        vm = [0]
        am = [0]
        ph = [0]

        def barrier():
            ph[0] += 5
            tgt = ph[0]

            @block.sync
            def _(s):
                s.drain()
                s.sem_inc(c.sem_ph, 1)
                s.wait_ge(c.sem_ph, tgt)

            @block.tensor
            def _(t):
                t.drain()
                t.sem_inc(c.sem_ph, 1)
                t.wait_ge(c.sem_ph, tgt)

            @block.vector
            def _(v):
                v.drain()
                v.sem_inc(c.sem_ph, 1)
                v.wait_ge(c.sem_ph, tgt)

            @block.scalar
            def _(a):
                a.drain()
                a.sem_inc(c.sem_ph, 1)
                a.wait_ge(c.sem_ph, tgt)

            @block.gpsimd
            def _(g):
                g.drain()
                g.sem_inc(c.sem_ph, 1)
                g.wait_ge(c.sem_ph, tgt)

        @block.gpsimd
        def _(g):
            g.dma_start(out=c.wk[:], in_=c.wk_t.rearrange("p a b o -> p (a b o)")
                        ).then_inc(c.sem_gq, 16)
            g.dma_start(out=c.wv[:], in_=c.wv_t.rearrange("p a b o -> p (a b o)")
                        ).then_inc(c.sem_gq, 16)
            g.dma_start(out=c.wg[:], in_=c.wg_t.rearrange("p a e -> p (a e)")
                        ).then_inc(c.sem_gq, 16)
        gq[0] += 48
        GQ_PRE = gq[0]

        @block.sync
        def _(s):
            s.dma_start(out=c.mask[:], in_=c.mask8).then_inc(c.sem_sq, 16)
            s.dma_start(out=c.ident[:], in_=c.ident_in).then_inc(c.sem_sq, 16)
        sq[0] += 32
        SQ_PRE = sq[0]

        @block.vector
        def _(v):
            v.memset(c.ones[:], 1.0)
            v.memset(c.onesrow[:], 1.0)
            v.memset(c.stat[:, c.ST_EPS:c.ST_EPS + 1], EPS)
            v.memset(c.stat[:, c.ST_NEGC:c.ST_NEGC + 1], -CSH)
            v.memset(c.stat[:, 44:48], 0.0).then_inc(c.sem_vm, 1)
        vm[0] += 1

        x_t = c.x_full.rearrange("(s j p) d -> s p j d", p=128, j=4)

        A_TM_PER_SWEEP = 64 + 32
        A_VM_PER_SWEEP = 4 * (2 + 16) + 32
        A_AM_PER_SWEEP = 4
        A_TM0, A_VM0, A_AM0, A_SQ0, A_GQ0 = tm[0], vm[0], am[0], sq[0], gq[0]

        @block.sync
        def _(s):
            for sw in range(16):
                if sw >= 1:
                    s.wait_ge(c.sem_tm, A_TM0 + (sw - 1) * A_TM_PER_SWEEP + 64)
                s.dma_start(out=c.xraw[:].rearrange("p (j d) -> p j d", j=4),
                            in_=x_t[sw]).then_inc(c.sem_sq, 16)
                for oc in range(DC):
                    s.wait_ge(c.sem_vm,
                              A_VM0 + sw * A_VM_PER_SWEEP + 72 + oc + 1)
                    slot = oc % 2
                    s.dma_start(
                        out=c.k_sc[oc * 128:(oc + 1) * 128, sw * 512:(sw + 1) * 512],
                        in_=c.kvst[:, slot * 512:(slot + 1) * 512],
                    ).then_inc(c.sem_sq, 16)
                for g in range(16):
                    t4, oc4 = divmod(g, 4)
                    s.wait_ge(c.sem_vm,
                              A_VM0 + sw * A_VM_PER_SWEEP + 88 + g + 1)
                    slot = 2 + g % 2
                    s.dma_start(
                        out=c.v_sc[oc4,
                                   sw * 512 + t4 * 128: sw * 512 + (t4 + 1) * 128,
                                   :],
                        in_=c.kvst[:, slot * 512:(slot + 1) * 512],
                    ).then_inc(c.sem_sq, 16)
        sq[0] += 16 * 33 * 16

        @block.scalar
        def _(a):
            for sw in range(16):
                for j in range(4):
                    a.wait_ge(c.sem_vm, A_VM0 + sw * A_VM_PER_SWEEP + j * 18 + 1)
                    a.activation(c.stat[:, c.ST_LNV:c.ST_LNV + 1],
                                 c.stat[:, c.ST_VAR2:c.ST_VAR2 + 1],
                                 AF.Ln, bias=c.stat[:, c.ST_EPS:c.ST_EPS + 1])
                    a.activation(c.stat[:, 45:46], c.stat[:, c.ST_EPS:c.ST_EPS + 1],
                                 AF.Square)
                    a.activation(c.stat[:, c.ST_RSTD:c.ST_RSTD + 1],
                                 c.stat[:, c.ST_LNV:c.ST_LNV + 1],
                                 AF.Exp, scale=-0.5).then_inc(c.sem_am, 1)
        am[0] += 16 * A_AM_PER_SWEEP

        @block.vector
        def _(v):
            st = c.stat
            for sw in range(16):
                v.wait_ge(c.sem_sq, A_SQ0 + (sw * 33 + 1) * 16)
                for j in range(4):
                    src = c.xraw[:, j * 2048:(j + 1) * 2048]
                    for q in range(4):
                        v.bn_stats(st[:, 16 + q * 6:22 + q * 6],
                                   src[:, q * 512:(q + 1) * 512])
                    v.tensor_copy(c.rscr[:, 0:512], c.rscr[:, 512:1024])
                    v.bn_aggr(st[:, 0:2], st[:, 16:40])
                    v.tensor_copy(c.rscr[:, 0:512], c.rscr[:, 512:1024])
                    v.tensor_copy(st[:, 12:14], st[:, 0:2]).then_inc(c.sem_vm, 1)
                    v.wait_ge(c.sem_am, A_AM0 + sw * A_AM_PER_SWEEP + j + 1)
                    v.tensor_scalar(c.xlnt[:], src,
                                    st[:, c.ST_MEAN2:c.ST_MEAN2 + 1],
                                    st[:, c.ST_RSTD:c.ST_RSTD + 1],
                                    OP.subtract, OP.mult).then_inc(c.sem_vm, 1)
                    for dcx in range(DC):
                        tr_i = j * 16 + dcx
                        v.wait_ge(c.sem_tm, A_TM0 + sw * A_TM_PER_SWEEP + tr_i + 1)
                        slotv = pbC[:].bitcast(BF16)
                        srcv = (slotv[:, 0:128] if tr_i % 2 == 0
                                else slotv[:, 1024:1152])
                        v.tensor_copy(
                            c.xlnf[:, dcx * 512 + j * 128: dcx * 512 + (j + 1) * 128],
                            srcv).then_inc(c.sem_vm, 1)
                for oc in range(DC):
                    v.wait_ge(c.sem_tm,
                              A_TM0 + sw * A_TM_PER_SWEEP + 64 + oc + 1)
                    half = oc % 2
                    if sw >= 1 or oc >= 2:
                        if oc >= 2:
                            prev_store = sw * 33 + oc
                        else:
                            prev_store = (sw - 1) * 33 + 16 + oc
                        v.wait_ge(c.sem_sq, A_SQ0 + prev_store * 16)
                    v.tensor_copy(
                        c.kvst[:, half * 512:(half + 1) * 512],
                        pbA[:, half * 512:(half + 1) * 512]).then_inc(c.sem_vm, 1)
                for g in range(16):
                    v.wait_ge(c.sem_tm,
                              A_TM0 + sw * A_TM_PER_SWEEP + 80 + g + 1)
                    half = g % 2
                    if sw >= 1 or g >= 2:
                        if g >= 2:
                            prev_store = sw * 33 + 16 + g
                        else:
                            prev_store = (sw - 1) * 33 + 32 + g
                        v.wait_ge(c.sem_sq, A_SQ0 + prev_store * 16)
                    v.tensor_copy(
                        c.kvst[:, (2 + half) * 512:(3 + half) * 512],
                        pbB[:, half * 512:(half + 1) * 512]).then_inc(c.sem_vm, 1)
        vm[0] += 16 * A_VM_PER_SWEEP

        @block.tensor
        def _(t):
            t.wait_ge(c.sem_gq, GQ_PRE)
            t.wait_ge(c.sem_sq, SQ_PRE)
            for sw in range(16):
                for j in range(4):
                    t.wait_ge(c.sem_vm, A_VM0 + sw * A_VM_PER_SWEEP + j * 18 + 2)
                    for dcx in range(DC):
                        tr_i = j * 16 + dcx
                        if tr_i >= 2:
                            t.wait_ge(c.sem_vm, A_VM0 + sw * A_VM_PER_SWEEP
                                      + _a_evac_vm(j * 16 + dcx - 2))
                        elif sw >= 1:
                            t.wait_ge(c.sem_vm, A_VM0 + (sw - 1) * A_VM_PER_SWEEP
                                      + _a_evac_vm(62 + tr_i))
                        slotv = pbC[:].bitcast(BF16)
                        dstv = (slotv[:, 0:128] if tr_i % 2 == 0
                                else slotv[:, 1024:1152])
                        t.transpose(dstv, c.xlnt[:, dcx * 128:(dcx + 1) * 128],
                                    c.ident[:]).then_inc(c.sem_tm, 1)
                t.wait_ge(c.sem_vm, A_VM0 + sw * A_VM_PER_SWEEP + 72)
                wk4 = c.wk[:].rearrange("p (a b o) -> p a b o", a=DC, b=DC)
                wv4 = c.wv[:].rearrange("p (a b o) -> p a b o", a=DC, b=DC)
                for oc in range(DC):
                    half = oc % 2
                    if oc >= 2:
                        t.wait_ge(c.sem_vm, A_VM0 + sw * A_VM_PER_SWEEP
                                  + 72 + oc - 2 + 1)
                    elif sw >= 1:
                        t.wait_ge(c.sem_vm, A_VM0 + (sw - 1) * A_VM_PER_SWEEP
                                  + 72 + 14 + oc + 1)
                    for dcx in range(DC):
                        ins = t.matmul(pbA[:, half * 512:(half + 1) * 512],
                                       wk4[:, dcx, oc, :],
                                       c.xlnf[:, dcx * 512:(dcx + 1) * 512],
                                       start=(dcx == 0), stop=(dcx == DC - 1))
                    ins.then_inc(c.sem_tm, 1)
                for g in range(16):
                    t4, oc4 = divmod(g, 4)
                    half = g % 2
                    if g >= 2:
                        t.wait_ge(c.sem_vm, A_VM0 + sw * A_VM_PER_SWEEP
                                  + 88 + g - 2 + 1)
                    elif sw >= 1:
                        t.wait_ge(c.sem_vm, A_VM0 + (sw - 1) * A_VM_PER_SWEEP
                                  + 88 + 14 + g + 1)
                    for dcx in range(DC):
                        ins = t.matmul(
                            pbB[:, half * 512:(half + 1) * 512],
                            c.xlnf[:, dcx * 512 + t4 * 128:
                                   dcx * 512 + (t4 + 1) * 128],
                            wv4[:, dcx, oc4 * 4:(oc4 + 1) * 4, :],
                            start=(dcx == 0), stop=(dcx == DC - 1))
                    ins.then_inc(c.sem_tm, 1)
        tm[0] += 16 * A_TM_PER_SWEEP

        if DEBUG:
            @block.sync
            def _(s):
                s.wait_ge(c.sem_vm, vm[0])
                s.dma_start(out=c.dbg_k, in_=c.k_sc).then_inc(c.sem_sq, 16)
                s.dma_start(out=c.dbg_v, in_=c.v_sc).then_inc(c.sem_sq, 16)
                s.dma_start(out=c.dbg_wk, in_=c.wk[:, 0:2048]).then_inc(c.sem_sq, 16)
                s.dma_start(out=c.dbg_xlnf, in_=c.xlnf[:]).then_inc(c.sem_sq, 16)
                s.dma_start(out=c.dbg_stat, in_=c.stat[:]).then_inc(c.sem_sq, 16)
                s.dma_start(out=c.dbg_xraw, in_=c.xraw[:]).then_inc(c.sem_sq, 16)
            sq[0] += 96

        @block.sync
        def _(s):
            s.wait_ge(c.sem_sq, sq[0])

        @block.gpsimd
        def _(g):
            g.wait_ge(c.sem_gq, gq[0])

        barrier()

        if "B" not in PHASES:
            return c.nc
        B_TM0, B_VM0, B_AM0, B_SQ0, B_GQ0 = tm[0], vm[0], am[0], sq[0], gq[0]
        B_VM_TILE = 2 + 16
        B_TM_TILE = 16

        @block.gpsimd
        def _(g):
            g.dma_start(out=c.wq[:], in_=c.wq_t.rearrange("p a b o -> p (a b o)")
                        ).then_inc(c.sem_gq, 16)
        gq[0] += 16

        @block.sync
        def _(s):
            for m in range(ML):
                if m >= 1:
                    s.wait_ge(c.sem_tm, B_TM0 + m * B_TM_TILE)
                s.dma_start(out=c.xrawb[:],
                            in_=c.x_shard[m * 128:(m + 1) * 128, :]
                            ).then_inc(c.sem_sq, 16)
        sq[0] += ML * 16

        @block.scalar
        def _(a):
            for m in range(ML):
                a.wait_ge(c.sem_vm, B_VM0 + m * B_VM_TILE + 1)
                a.activation(c.stat[:, c.ST_LNV:c.ST_LNV + 1],
                             c.stat[:, c.ST_VAR2:c.ST_VAR2 + 1],
                             AF.Ln, bias=c.stat[:, c.ST_EPS:c.ST_EPS + 1])
                a.activation(c.stat[:, 45:46], c.stat[:, c.ST_EPS:c.ST_EPS + 1],
                             AF.Square)
                a.activation(c.stat[:, c.ST_RSTD:c.ST_RSTD + 1],
                             c.stat[:, c.ST_LNV:c.ST_LNV + 1],
                             AF.Exp, scale=-0.5).then_inc(c.sem_am, 1)
        am[0] += ML

        @block.vector
        def _(v):
            st = c.stat
            for m in range(ML):
                v.wait_ge(c.sem_sq, B_SQ0 + (m + 1) * 16)
                src = c.xrawb[:]
                for q in range(4):
                    v.bn_stats(st[:, 16 + q * 6:22 + q * 6],
                               src[:, q * 512:(q + 1) * 512])
                v.tensor_copy(c.rscr[:, 0:512], c.rscr[:, 512:1024])
                v.bn_aggr(st[:, 0:2], st[:, 16:40])
                v.tensor_copy(c.rscr[:, 0:512], c.rscr[:, 512:1024])
                v.tensor_copy(st[:, 12:14], st[:, 0:2]).then_inc(c.sem_vm, 1)
                v.wait_ge(c.sem_am, B_AM0 + m + 1)
                v.tensor_scalar(c.xlntb[:], src,
                                st[:, c.ST_MEAN2:c.ST_MEAN2 + 1],
                                st[:, c.ST_RSTD:c.ST_RSTD + 1],
                                OP.subtract, OP.mult).then_inc(c.sem_vm, 1)
                for dcx in range(DC):
                    v.wait_ge(c.sem_tm, B_TM0 + m * B_TM_TILE + dcx + 1)
                    slotv = pbC[:].bitcast(BF16)
                    srcv = (slotv[:, 0:128] if dcx % 2 == 0
                            else slotv[:, 1024:1152])
                    v.tensor_copy(
                        c.xlnof[:, dcx * 1024 + m * 128: dcx * 1024 + (m + 1) * 128],
                        srcv).then_inc(c.sem_vm, 1)
            for oc in range(NH):
                pb = pbA if oc % 2 == 0 else pbB
                v.wait_ge(c.sem_tm, B_TM0 + ML * B_TM_TILE + oc + 1)
                v.tensor_copy(c.qfm[:, oc * 1024:oc * 1024 + 512],
                              pb[:, 0:512]).then_inc(c.sem_vm, 1)
                v.tensor_copy(c.qfm[:, oc * 1024 + 512:(oc + 1) * 1024],
                              pb[:, 512:1024]).then_inc(c.sem_vm, 1)
        vm[0] += ML * B_VM_TILE + NH * 2

        @block.tensor
        def _(t):
            t.wait_ge(c.sem_gq, gq[0])
            for m in range(ML):
                t.wait_ge(c.sem_vm, B_VM0 + m * B_VM_TILE + 2)
                for dcx in range(DC):
                    tr_i = m * 16 + dcx
                    if tr_i >= 2:
                        pm, pd = divmod(tr_i - 2, 16)
                        t.wait_ge(c.sem_vm, B_VM0 + pm * B_VM_TILE + 2 + pd + 1)
                    slotv = pbC[:].bitcast(BF16)
                    dstv = (slotv[:, 0:128] if dcx % 2 == 0
                            else slotv[:, 1024:1152])
                    t.transpose(dstv, c.xlntb[:, dcx * 128:(dcx + 1) * 128],
                                c.ident[:]).then_inc(c.sem_tm, 1)
            wq4 = c.wq[:].rearrange("p (a b o) -> p a b o", a=DC, b=DC)
            t.wait_ge(c.sem_vm, B_VM0 + ML * B_VM_TILE)
            for oc in range(NH):
                pb = pbA if oc % 2 == 0 else pbB
                if oc >= 2:
                    t.wait_ge(c.sem_vm, B_VM0 + ML * B_VM_TILE + (oc - 2) * 2 + 2)
                for dcx in range(DC):
                    for tch in range(2):
                        ins = t.matmul(pb[:, tch * 512:(tch + 1) * 512],
                                       wq4[:, dcx, oc, :],
                                       c.xlnof[:, dcx * 1024 + tch * 512:
                                               dcx * 1024 + (tch + 1) * 512],
                                       start=(dcx == 0), stop=(dcx == DC - 1))
                ins.then_inc(c.sem_tm, 1)
        tm[0] += ML * B_TM_TILE + NH

        if DEBUG:
            @block.sync
            def _(s):
                s.wait_ge(c.sem_vm, vm[0])
                s.dma_start(out=c.dbg_q, in_=c.qfm[:]).then_inc(c.sem_sq, 16)
            sq[0] += 16

        @block.sync
        def _(s):
            s.wait_ge(c.sem_sq, sq[0])

        @block.gpsimd
        def _(g):
            g.wait_ge(c.sem_gq, gq[0])

        barrier()

        if "C" not in PHASES:
            return c.nc
        C_TM0, C_VM0, C_AM0, C_SQ0 = tm[0], vm[0], am[0], sq[0]
        C_TM_H = 64 + 64 + 1
        C_VM_H = 2
        C_AM_H = 64

        def c_tm_st(h, jj):
            return C_TM0 + h * C_TM_H + (1 if jj == 0 else 2 * jj)

        def c_tm_lpv(h, jj):
            if jj == 63:
                return C_TM0 + h * C_TM_H + 128
            return C_TM0 + h * C_TM_H + 2 * jj + 3

        def c_tm_bcast(h):
            return C_TM0 + (h + 1) * C_TM_H

        def c_vm_lrec(h):
            return C_VM0 + h * C_VM_H + 1

        def c_vm_zev(h):
            return C_VM0 + (h + 1) * C_VM_H

        def c_am_exp(h, jj):
            return C_AM0 + h * C_AM_H + jj + 1

        def parts_of(jj):
            a0 = (jj // 8) * 128
            p = []
            if a0 < 512:
                p.append((a0, 512))
            p.append((max(a0, 512), 1024))
            return p

        @block.sync
        def _(s):
            for h in range(NH):
                if h >= 2:
                    s.wait_ge(c.sem_tm, c_tm_lpv(h - 2, 63))
                s.dma_start(out=c.kbuf[:, (h % 2) * 8192:(h % 2 + 1) * 8192],
                            in_=c.k_sc[h * 128:(h + 1) * 128, :]
                            ).then_inc(c.sem_sq, 16)
                for b8 in range(8):
                    s.dma_start(
                        out=c.vtm[:, (h % 2) * 8192 + b8 * 1024:
                                  (h % 2) * 8192 + (b8 + 1) * 1024]
                        .rearrange("p (jj d) -> p jj d", d=128),
                        in_=c.v_sc[h // 4, b8 * 1024:(b8 + 1) * 1024,
                                   (h % 4) * 128:(h % 4 + 1) * 128]
                        .rearrange("(jj p) d -> p jj d", p=128)
                    ).then_inc(c.sem_sq, 16)
        sq[0] += NH * 16 * 9

        @block.tensor
        def _(t):
            for h in range(NH):
                kb = c.kbuf[:, (h % 2) * 8192:(h % 2 + 1) * 8192]
                vt = c.vtm[:, (h % 2) * 8192:(h % 2 + 1) * 8192]
                t.wait_ge(c.sem_sq, C_SQ0 + 9 * (h + 1) * 16)
                if h >= 1:
                    t.wait_ge(c.sem_am, C_AM0 + h * C_AM_H)
                    t.wait_ge(c.sem_vm, c_vm_zev(h - 1))
                qh = c.qfm[:, h * 1024:(h + 1) * 1024]

                def emit_st(jj):
                    stp = pbC if jj % 2 == 0 else pbD
                    if jj >= 2:
                        t.wait_ge(c.sem_am, c_am_exp(h, jj - 2))
                    a0 = (jj // 8) * 128
                    for (p0, p1) in parts_of(jj):
                        first_chunk = p0 <= a0 < p1
                        t.matmul(stp[:, p0:p1],
                                 kb[:, jj * 128:(jj + 1) * 128],
                                 qh[:, p0:p1], start=True,
                                 stop=not first_chunk)
                    r = jj % 8
                    ins = t.matmul(stp[:, a0:a0 + 128],
                                   c.mask[:, r * 128:(r + 1) * 128],
                                   c.ident[:], start=False, stop=True)
                    ins.then_inc(c.sem_tm, 1)

                def emit_lpv(jj):
                    ptp = c.ptb[:, (jj % 2) * 1024:(jj % 2 + 1) * 1024]
                    t.wait_ge(c.sem_am, c_am_exp(h, jj))
                    first = (jj == 0)
                    last = (jj == 63)
                    for (p0, p1) in parts_of(jj):
                        t.matmul(pbB[0:1, p0:p1], c.ones[:], ptp[:, p0:p1],
                                 start=first, stop=last)
                    for (p0, p1) in parts_of(jj):
                        ins = t.matmul(pbA[:, p0:p1],
                                       vt[:, jj * 128:(jj + 1) * 128],
                                       ptp[:, p0:p1], start=first, stop=last)
                    ins.then_inc(c.sem_tm, 1)

                emit_st(0)
                for jj in range(1, NT):
                    emit_st(jj)
                    emit_lpv(jj - 1)
                emit_lpv(63)
                t.wait_ge(c.sem_vm, c_vm_lrec(h))
                t.wait_ge(c.sem_am, c_am_exp(h, 63))
                for tch in range(2):
                    ins = t.matmul(pbC[:, tch * 512:(tch + 1) * 512],
                                   c.onesrow_f[:],
                                   c.lrow2[0:1, tch * 512:(tch + 1) * 512],
                                   start=True, stop=True)
                ins.then_inc(c.sem_tm, 1)
        tm[0] += NH * C_TM_H

        @block.scalar
        def _(a):
            for h in range(NH):
                for jj in range(NT):
                    stp = pbC if jj % 2 == 0 else pbD
                    ptp = c.ptb[:, (jj % 2) * 1024:(jj % 2 + 1) * 1024]
                    a0 = (jj // 8) * 128
                    a.wait_ge(c.sem_tm, c_tm_st(h, jj))
                    if jj >= 2:
                        a.wait_ge(c.sem_tm, c_tm_lpv(h, jj - 2))
                    elif h >= 1:
                        a.wait_ge(c.sem_tm, c_tm_lpv(h - 1, 62 + jj))
                    a.activation(ptp[:, a0:1024], stp[:, a0:1024], AF.Exp,
                                 bias=c.stat[:, c.ST_NEGC:c.ST_NEGC + 1],
                                 scale=ISQ).then_inc(c.sem_am, 1)
        am[0] += NH * C_AM_H

        @block.vector
        def _(v):
            v.memset(c.onesrow_f[:], 1.0)
            for h in range(NH):
                v.wait_ge(c.sem_tm, c_tm_lpv(h, 63))
                v.tensor_copy(c.lrow[0:1, :], pbB[0:1, :])
                v.reciprocal(c.lrow[0:1, :], c.lrow[0:1, :])
                v.tensor_copy(c.rscr[:, 0:512], c.rscr[:, 512:1024])
                v.tensor_copy(c.lrow2[0:1, :], c.lrow[0:1, :]).then_inc(c.sem_vm, 1)
                v.wait_ge(c.sem_tm, c_tm_bcast(h))
                v.tensor_copy(c.rscr[:], pbC[:])
                v.tensor_mul(
                    c.zfm[:, h * 1024: h * 1024 + 512],
                    pbA[:, 0:512], c.rscr[:, 0:512])
                v.tensor_mul(
                    c.zfm[:, h * 1024 + 512: (h + 1) * 1024],
                    pbA[:, 512:1024], c.rscr[:, 512:1024]).then_inc(c.sem_vm, 1)
        vm[0] += NH * C_VM_H

        if DEBUG:
            @block.sync
            def _(s):
                s.wait_ge(c.sem_vm, vm[0])
                s.dma_start(out=c.dbg_z, in_=c.zfm[:]).then_inc(c.sem_sq, 16)
                s.dma_start(out=c.dbg_vtm, in_=c.vtm[:, 8192:16384]).then_inc(c.sem_sq, 16)
                s.dma_start(out=c.dbg_lrow, in_=c.lrow[0:1, :]).then_inc(c.sem_sq, 16)
                s.dma_start(out=c.dbg_rscr, in_=c.rscr[:]).then_inc(c.sem_sq, 16)
            sq[0] += 64

        @block.sync
        def _(s):
            s.wait_ge(c.sem_sq, sq[0])

        barrier()

        if "D" not in PHASES:
            return c.nc
        D_TM0, D_VM0, D_AM0, D_SQ0, D_GQ0 = tm[0], vm[0], am[0], sq[0], gq[0]

        @block.gpsimd
        def _(g):
            g.dma_start(out=c.wo[:].rearrange("p (a o) -> p a o", a=DC),
                        in_=c.wo_t.rearrange("a p o -> p a o")
                        ).then_inc(c.sem_gq, 16)
        gq[0] += 16

        D_TM_M = 4 + 16
        D_VM_M = 5 + 16
        D_AM_M = 2

        @block.sync
        def _(s):
            for m in range(ML):
                s.dma_start(out=c.xrawd[:],
                            in_=c.x_shard[m * 128:(m + 1) * 128, :]
                            ).then_inc(c.sem_sq, 16)
                s.wait_ge(c.sem_vm, D_VM0 + m * D_VM_M + 3)
                s.dma_start(out=c.x2_sc[m * 128:(m + 1) * 128, :],
                            in_=c.x2t[:]).then_inc(c.sem_sq, 16)
        sq[0] += ML * 32

        @block.tensor
        def _(t):
            t.wait_ge(c.sem_gq, gq[0])
            wo3 = c.wo[:].rearrange("p (a o) -> p a o", a=DC)
            for m in range(ML):
                for nc4 in range(4):
                    pb = (pbA, pbA, pbB, pbB)[nc4]
                    half = nc4 % 2
                    if m >= 1:
                        t.wait_ge(c.sem_vm, D_VM0 + (m - 1) * D_VM_M + 1)
                    for zc in range(DC):
                        ins = t.matmul(pb[:, half * 512:(half + 1) * 512],
                                       c.zfm[:, zc * 1024 + m * 128:
                                             zc * 1024 + (m + 1) * 128],
                                       wo3[:, zc, nc4 * 512:(nc4 + 1) * 512],
                                       start=(zc == 0), stop=(zc == DC - 1))
                    ins.then_inc(c.sem_tm, 1)
                t.wait_ge(c.sem_vm, D_VM0 + m * D_VM_M + 5)
                for dcx in range(DC):
                    tr_i = m * 16 + dcx
                    if tr_i >= 2:
                        pm, pd = divmod(tr_i - 2, 16)
                        t.wait_ge(c.sem_vm, D_VM0 + pm * D_VM_M + 5 + pd + 1)
                    slotv = pbD[:].bitcast(BF16)
                    dstv = (slotv[:, 0:128] if dcx % 2 == 0
                            else slotv[:, 1024:1152])
                    t.transpose(dstv, c.xtt[:, dcx * 128:(dcx + 1) * 128],
                                c.ident[:]).then_inc(c.sem_tm, 1)
            wg3 = c.wg[:].rearrange("p (a e) -> p a e", a=DC)
            t.wait_ge(c.sem_vm, D_VM0 + ML * D_VM_M)
            for m in range(ML):
                for dcx in range(DC):
                    ins = t.matmul(pbC[:, m * 16:(m + 1) * 16],
                                   c.xtf[:, dcx * 1024 + m * 128:
                                         dcx * 1024 + (m + 1) * 128],
                                   wg3[:, dcx, :],
                                   start=(dcx == 0), stop=(dcx == DC - 1))
            ins.then_inc(c.sem_tm, 1)
        tm[0] += ML * D_TM_M + 1

        @block.scalar
        def _(a):
            for m in range(ML):
                a.wait_ge(c.sem_vm, D_VM0 + m * D_VM_M + 2)
                a.activation(c.stat[:, c.ST_LNV:c.ST_LNV + 1],
                             c.stat[:, c.ST_VAR2:c.ST_VAR2 + 1],
                             AF.Ln, bias=c.stat[:, c.ST_EPS:c.ST_EPS + 1])
                a.activation(c.stat[:, 45:46], c.stat[:, c.ST_EPS:c.ST_EPS + 1],
                             AF.Square)
                a.activation(c.stat[:, c.ST_RSTD:c.ST_RSTD + 1],
                             c.stat[:, c.ST_LNV:c.ST_LNV + 1],
                             AF.Exp, scale=-0.5).then_inc(c.sem_am, 1)
                a.wait_ge(c.sem_vm, D_VM0 + m * D_VM_M + 4)
                a.activation(c.stat[:, c.ST_LNV:c.ST_LNV + 1],
                             c.stat[:, c.ST_VAR2:c.ST_VAR2 + 1],
                             AF.Ln, bias=c.stat[:, c.ST_EPS:c.ST_EPS + 1])
                a.activation(c.stat[:, 45:46], c.stat[:, c.ST_EPS:c.ST_EPS + 1],
                             AF.Square)
                a.activation(c.stat[:, c.ST_RSTD:c.ST_RSTD + 1],
                             c.stat[:, c.ST_LNV:c.ST_LNV + 1],
                             AF.Exp, scale=-0.5).then_inc(c.sem_am, 1)
            a.wait_ge(c.sem_vm, D_VM0 + ML * D_VM_M + 1)
            for m in range(ML):
                g8 = c.gsc8[:, m * 96:(m + 1) * 96]
                ins = a.activation(g8[:, 0:16], pbC[:, m * 16:(m + 1) * 16],
                                   AF.Exp, bias=g8[:, 84:85])
            ins.then_inc(c.sem_am, 1)
        am[0] += ML * D_AM_M + 1

        @block.vector
        def _(v):
            st = c.stat
            for m in range(ML):
                for nc4 in range(4):
                    pb = (pbA, pbA, pbB, pbB)[nc4]
                    half = nc4 % 2
                    v.wait_ge(c.sem_tm, D_TM0 + m * D_TM_M + nc4 + 1)
                    ins = v.tensor_copy(c.aout[:, nc4 * 512:(nc4 + 1) * 512],
                                        pb[:, half * 512:(half + 1) * 512])
                ins.then_inc(c.sem_vm, 1)
                for q in range(4):
                    v.bn_stats(st[:, 16 + q * 6:22 + q * 6],
                               c.aout[:, q * 512:(q + 1) * 512])
                v.tensor_copy(c.rscr[:, 0:512], c.rscr[:, 512:1024])
                v.bn_aggr(st[:, 0:2], st[:, 16:40])
                v.tensor_copy(c.rscr[:, 0:512], c.rscr[:, 512:1024])
                v.tensor_copy(st[:, 12:14], st[:, 0:2]).then_inc(c.sem_vm, 1)
                v.wait_ge(c.sem_am, D_AM0 + m * D_AM_M + 1)
                v.tensor_scalar(c.x2t[:], c.aout[:],
                                st[:, c.ST_MEAN2:c.ST_MEAN2 + 1],
                                st[:, c.ST_RSTD:c.ST_RSTD + 1],
                                OP.subtract, OP.mult)
                v.wait_ge(c.sem_sq, D_SQ0 + (m * 2 + 1) * 16)
                v.tensor_add(c.x2t[:], c.x2t[:], c.xrawd[:]).then_inc(c.sem_vm, 1)
                for q in range(4):
                    v.bn_stats(st[:, 16 + q * 6:22 + q * 6],
                               c.x2t[:, q * 512:(q + 1) * 512])
                v.tensor_copy(c.rscr[:, 0:512], c.rscr[:, 512:1024])
                v.bn_aggr(st[:, 0:2], st[:, 16:40])
                v.tensor_copy(c.rscr[:, 0:512], c.rscr[:, 512:1024])
                v.tensor_copy(st[:, 12:14], st[:, 0:2]).then_inc(c.sem_vm, 1)
                v.wait_ge(c.sem_am, D_AM0 + m * D_AM_M + 2)
                v.tensor_scalar(c.xtt[:], c.x2t[:],
                                st[:, c.ST_MEAN2:c.ST_MEAN2 + 1],
                                st[:, c.ST_RSTD:c.ST_RSTD + 1],
                                OP.subtract, OP.mult).then_inc(c.sem_vm, 1)
                for dcx in range(DC):
                    v.wait_ge(c.sem_tm, D_TM0 + m * D_TM_M + 4 + dcx + 1)
                    slotv = pbD[:].bitcast(BF16)
                    srcv = (slotv[:, 0:128] if dcx % 2 == 0
                            else slotv[:, 1024:1152])
                    v.tensor_copy(
                        c.xtf[:, dcx * 1024 + m * 128: dcx * 1024 + (m + 1) * 128],
                        srcv).then_inc(c.sem_vm, 1)
            G = lambda m: c.gsc8[:, m * 96:(m + 1) * 96]
            v.wait_ge(c.sem_tm, D_TM0 + ML * D_TM_M + 1)
            for m in range(ML):
                ins = v.reduce_max(G(m)[:, 84:85], pbC[:, m * 16:(m + 1) * 16],
                                   axis=AX.X, negate=True)
            ins.then_inc(c.sem_vm, 1)
            v.wait_ge(c.sem_am, D_AM0 + ML * D_AM_M + 1)
            for m in range(ML):
                v.reduce_sum(G(m)[:, 85:86], G(m)[:, 0:16], axis=AX.X)
            for m in range(ML):
                v.reciprocal(G(m)[:, 86:87], G(m)[:, 85:86])
            for m in range(ML):
                v.reduce_max(G(m)[:, 16:20],
                             G(m)[:, 0:16].rearrange("p (a b) -> p a b", b=4),
                             axis=AX.X)
            for gi in range(4):
                for m in range(ML):
                    dst = G(m)[:, 20:24] if gi == 0 else G(m)[:, 24:28]
                    v.tensor_scalar(dst, G(m)[:, 16:20],
                                    G(m)[:, 16 + gi:17 + gi], None, OP.is_le)
                if gi > 0:
                    for m in range(ML):
                        v.tensor_add(G(m)[:, 20:24], G(m)[:, 20:24],
                                     G(m)[:, 24:28])
            for m in range(ML):
                v.tensor_scalar(G(m)[:, 28:32], G(m)[:, 20:24], 2.0, None,
                                OP.is_le)
            for m in range(ML):
                v.tensor_scalar(G(m)[:, 32:36], G(m)[:, 28:32], 1e4, -1e4,
                                OP.mult, OP.add)
            for e in range(E):
                gi = e // 4
                for m in range(ML):
                    v.scalar_tensor_tensor(G(m)[:, 36 + e:37 + e],
                                           G(m)[:, e:e + 1],
                                           G(m)[:, 28 + gi:29 + gi],
                                           G(m)[:, 32 + gi:33 + gi],
                                           OP.mult, OP.add)
            for f in range(E):
                for m in range(ML):
                    dst = G(m)[:, 52:68] if f == 0 else G(m)[:, 68:84]
                    v.tensor_scalar(dst, G(m)[:, 36:52],
                                    G(m)[:, 36 + f:37 + f], None, OP.is_le)
                if f > 0:
                    for m in range(ML):
                        v.tensor_add(G(m)[:, 52:68], G(m)[:, 52:68],
                                     G(m)[:, 68:84])
            for m in range(ML):
                v.tensor_scalar(G(m)[:, 68:84], G(m)[:, 52:68], 4.0, None,
                                OP.is_le)
            for m in range(ML):
                v.tensor_mul(c.comb[:, m * 16:(m + 1) * 16], G(m)[:, 0:16],
                             G(m)[:, 68:84])
            for m in range(ML):
                ins = v.tensor_scalar_mul(c.comb[:, m * 16:(m + 1) * 16],
                                          c.comb[:, m * 16:(m + 1) * 16],
                                          G(m)[:, 86:87])
            ins.then_inc(c.sem_vm, 1)
        vm[0] += ML * D_VM_M + 2

        if DEBUG:
            @block.sync
            def _(s):
                s.wait_ge(c.sem_vm, vm[0])
                s.wait_ge(c.sem_am, am[0])
                s.dma_start(out=c.dbg_comb, in_=c.comb[:]).then_inc(c.sem_sq, 16)
                s.dma_start(out=c.dbg_xtf, in_=c.xtf[:]).then_inc(c.sem_sq, 16)
                s.dma_start(out=c.dbg_x2, in_=c.x2_sc).then_inc(c.sem_sq, 16)
                s.dma_start(out=c.dbg_gsc, in_=c.gsc[:]).then_inc(c.sem_sq, 16)
            sq[0] += 64

        @block.sync
        def _(s):
            s.wait_ge(c.sem_sq, sq[0])

        @block.gpsimd
        def _(g):
            g.wait_ge(c.sem_gq, gq[0])

        barrier()

        if "E" not in PHASES:
            return c.nc
        E_TM0, E_VM0, E_AM0, E_GQ0 = tm[0], vm[0], am[0], gq[0]
        NE = E + NSH
        E_TM_E = ICN * 4 + ML * 4
        E_VM_E = ICN * 2 + ML * 4
        E_AM_E = ICN * 2

        def e_tm_hgrp(e, ic, grp):
            return E_TM0 + e * E_TM_E + ic * 4 + grp + 1

        def e_tm_w2(e, m, nc4):
            return E_TM0 + e * E_TM_E + ICN * 4 + m * 4 + nc4 + 1

        def e_vm_hev(e, ic, tch):
            return E_VM0 + 1 + e * E_VM_E + ic * 2 + tch + 1

        def e_vm_w2ev(e, m, nc4):
            return E_VM0 + 1 + e * E_VM_E + ICN * 2 + m * 4 + nc4 + 1

        def e_am_sil(e, ic, tch):
            return E_AM0 + e * E_AM_E + ic * 2 + tch + 1

        def ew(e):
            if e < E:
                return c.w1_t[e], c.w3_t[e], c.w2_t[e]
            return c.ws1_t[e - E], c.ws3_t[e - E], c.ws2_t[e - E]

        @block.gpsimd
        def _(g):
            for e in range(NE):
                w1s, w3s, w2s = ew(e)
                for ic in range(ICN):
                    par = ic % 2
                    if e > 0 or ic >= 2:
                        pe, pic = (e, ic - 2) if ic >= 2 else (e - 1, ICN - 2 + ic)
                        if ic < 2:
                            pic = ICN - 1 if ic == 0 else ICN - 2
                        g.wait_ge(c.sem_tm, e_tm_hgrp(pe, pic, 3))
                    g.dma_start(out=c.w1b[:, par * 2048:(par + 1) * 2048],
                                in_=w1s[ic].rearrange("p a i -> p (a i)")
                                ).then_inc(c.sem_gq, 16)
                    g.dma_start(out=c.w3b[:, par * 2048:(par + 1) * 2048],
                                in_=w3s[ic].rearrange("p a i -> p (a i)")
                                ).then_inc(c.sem_gq, 16)
                if e > 0:
                    g.wait_ge(c.sem_tm, e_tm_w2(e - 1, ML - 1, 3))
                for ic in range(ICN):
                    g.dma_start(out=c.w2a[:, ic * 2048:(ic + 1) * 2048],
                                in_=w2s[ic]).then_inc(c.sem_gq, 16)
        gq[0] += NE * ICN * 48

        @block.tensor
        def _(t):
            for e in range(NE):
                for ic in range(ICN):
                    par = ic % 2
                    t.wait_ge(c.sem_gq, E_GQ0 + (e * ICN + ic + 1) * 32
                              + e * ICN * 16)
                    for mat in range(2):
                        wb = c.w1b if mat == 0 else c.w3b
                        pb = pbA if mat == 0 else pbB
                        for tch in range(2):
                            if ic >= 1:
                                if mat == 0:
                                    t.wait_ge(c.sem_am, e_am_sil(e, ic - 1, tch))
                                else:
                                    t.wait_ge(c.sem_vm, e_vm_hev(e, ic - 1, tch))
                            elif e > 0:
                                if mat == 0:
                                    t.wait_ge(c.sem_am, e_am_sil(e - 1, ICN - 1, tch))
                                else:
                                    t.wait_ge(c.sem_vm, e_vm_hev(e - 1, ICN - 1, tch))
                        last = {}
                        for dcx in range(DC):
                            for tch in range(2):
                                last[tch] = t.matmul(
                                    pb[:, tch * 512:(tch + 1) * 512],
                                    wb[:, par * 2048 + dcx * 128:
                                       par * 2048 + (dcx + 1) * 128],
                                    c.xtf[:, dcx * 1024 + tch * 512:
                                          dcx * 1024 + (tch + 1) * 512],
                                    start=(dcx == 0), stop=(dcx == DC - 1))
                        last[0].then_inc(c.sem_tm, 1)
                        last[1].then_inc(c.sem_tm, 1)
                t.wait_ge(c.sem_gq, E_GQ0 + (e + 1) * ICN * 48)
                for m in range(ML):
                    for nc4 in range(4):
                        if m >= 1:
                            t.wait_ge(c.sem_vm, e_vm_w2ev(e, m - 1, nc4))
                        elif e > 0:
                            t.wait_ge(c.sem_vm, e_vm_w2ev(e - 1, ML - 1, nc4))
                    last = {}
                    for ic in range(ICN):
                        for nc4 in range(4):
                            pb = (pbC, pbC, pbD, pbD)[nc4]
                            half = nc4 % 2
                            last[nc4] = t.matmul(
                                pb[:, half * 512:(half + 1) * 512],
                                c.h[:, ic * 1024 + m * 128:
                                    ic * 1024 + (m + 1) * 128],
                                c.w2a[:, ic * 2048 + nc4 * 512:
                                      ic * 2048 + (nc4 + 1) * 512],
                                start=(ic == 0), stop=(ic == ICN - 1))
                    for nc4 in range(4):
                        last[nc4].then_inc(c.sem_tm, 1)
        tm[0] += NE * E_TM_E

        @block.scalar
        def _(a):
            for e in range(NE):
                for ic in range(ICN):
                    for tch in range(2):
                        a.wait_ge(c.sem_tm, e_tm_hgrp(e, ic, tch))
                        if ic >= 1:
                            a.wait_ge(c.sem_vm, e_vm_hev(e, ic - 1, tch))
                        elif e > 0:
                            a.wait_ge(c.sem_vm, e_vm_hev(e - 1, ICN - 1, tch))
                        a.activation(c.sil[:, tch * 512:(tch + 1) * 512],
                                     pbA[:, tch * 512:(tch + 1) * 512],
                                     AF.Silu).then_inc(c.sem_am, 1)
        am[0] += NE * E_AM_E

        @block.vector
        def _(v):
            v.memset(c.y[:], 0.0).then_inc(c.sem_vm, 1)
            for e in range(NE):
                for ic in range(ICN):
                    for tch in range(2):
                        v.wait_ge(c.sem_am, e_am_sil(e, ic, tch))
                        v.wait_ge(c.sem_tm, e_tm_hgrp(e, ic, 2 + tch))
                        v.tensor_mul(
                            c.h[:, ic * 1024 + tch * 512:
                                ic * 1024 + (tch + 1) * 512],
                            pbB[:, tch * 512:(tch + 1) * 512],
                            c.sil[:, tch * 512:(tch + 1) * 512]
                            ).then_inc(c.sem_vm, 1)
                for m in range(ML):
                    for nc4 in range(4):
                        pb = (pbC, pbC, pbD, pbD)[nc4]
                        half = nc4 % 2
                        v.wait_ge(c.sem_tm, e_tm_w2(e, m, nc4))
                        ysl = c.y[:, m * 2048 + nc4 * 512: m * 2048 + (nc4 + 1) * 512]
                        if e < E:
                            v.scalar_tensor_tensor(
                                ysl, pb[:, half * 512:(half + 1) * 512],
                                c.comb[:, m * 16 + e: m * 16 + e + 1],
                                ysl, OP.mult, OP.add).then_inc(c.sem_vm, 1)
                        else:
                            v.tensor_add(ysl, pb[:, half * 512:(half + 1) * 512],
                                         ysl).then_inc(c.sem_vm, 1)
        vm[0] += 1 + NE * E_VM_E

        if DEBUG:
            @block.sync
            def _(s):
                s.wait_ge(c.sem_vm, vm[0])
                s.dma_start(out=c.dbg_y, in_=c.y[:]).then_inc(c.sem_sq, 16)
            sq[0] += 16

        @block.gpsimd
        def _(g):
            g.wait_ge(c.sem_gq, gq[0])

        barrier()

        if "F" not in PHASES:
            return c.nc
        F_TM0, F_VM0, F_AM0, F_SQ0 = tm[0], vm[0], am[0], sq[0]
        F_VM_M = 3
        F_AM_M = 1

        @block.sync
        def _(s):
            for m in range(ML):
                if m >= 1:
                    s.wait_ge(c.sem_vm, F_VM0 + m * F_VM_M)
                s.dma_start(out=c.x2f[:],
                            in_=c.x2_sc[m * 128:(m + 1) * 128, :]
                            ).then_inc(c.sem_sq, 16)
                s.wait_ge(c.sem_vm, F_VM0 + m * F_VM_M + 3)
                s.dma_start(out=c.out[m * 128:(m + 1) * 128, :],
                            in_=c.fout[:]).then_inc(c.sem_sq, 16)
            s.wait_ge(c.sem_sq, sq[0] + ML * 32)
        sq[0] += ML * 32

        @block.scalar
        def _(a):
            for m in range(ML):
                a.wait_ge(c.sem_vm, F_VM0 + m * F_VM_M + 1)
                a.activation(c.stat[:, c.ST_LNV:c.ST_LNV + 1],
                             c.stat[:, c.ST_VAR2:c.ST_VAR2 + 1],
                             AF.Ln, bias=c.stat[:, c.ST_EPS:c.ST_EPS + 1])
                a.activation(c.stat[:, 45:46], c.stat[:, c.ST_EPS:c.ST_EPS + 1],
                             AF.Square)
                a.activation(c.stat[:, c.ST_RSTD:c.ST_RSTD + 1],
                             c.stat[:, c.ST_LNV:c.ST_LNV + 1],
                             AF.Exp, scale=-0.5).then_inc(c.sem_am, 1)
        am[0] += ML * F_AM_M

        @block.vector
        def _(v):
            st = c.stat
            for m in range(ML):
                ysl = c.y[:, m * 2048:(m + 1) * 2048]
                for q in range(4):
                    v.bn_stats(st[:, 16 + q * 6:22 + q * 6],
                               ysl[:, q * 512:(q + 1) * 512])
                v.tensor_copy(c.rscr[:, 0:512], c.rscr[:, 512:1024])
                v.bn_aggr(st[:, 0:2], st[:, 16:40])
                v.tensor_copy(c.rscr[:, 0:512], c.rscr[:, 512:1024])
                v.tensor_copy(st[:, 12:14], st[:, 0:2]).then_inc(c.sem_vm, 1)
                v.wait_ge(c.sem_am, F_AM0 + m * F_AM_M + 1)
                if m >= 1:
                    v.wait_ge(c.sem_sq, F_SQ0 + m * 32)
                v.tensor_scalar(c.fout[:], ysl,
                                st[:, c.ST_MEAN2:c.ST_MEAN2 + 1],
                                st[:, c.ST_RSTD:c.ST_RSTD + 1],
                                OP.subtract, OP.mult).then_inc(c.sem_vm, 1)
                v.wait_ge(c.sem_sq, F_SQ0 + (m * 2 + 1) * 16)
                v.tensor_add(c.fout[:], c.fout[:], c.x2f[:]).then_inc(c.sem_vm, 1)
        vm[0] += ML * F_VM_M

        barrier()
    return c.nc


def _a_evac_vm(tr_i):
    j, dcx = divmod(tr_i, 16)
    return j * 18 + 2 + dcx + 1


def _prep_weights(inp):
    f32 = np.float32
    wq = np.ascontiguousarray(inp["L0_attn_Wq"], dtype=f32)
    wk = np.ascontiguousarray(inp["L0_attn_Wk"], dtype=f32)
    wv = np.ascontiguousarray(inp["L0_attn_Wv"], dtype=f32)
    wo = np.ascontiguousarray(inp["L0_attn_Wo"], dtype=f32)
    wg = np.ascontiguousarray(inp["L0_ffn_Wg"], dtype=f32)
    w1 = np.ascontiguousarray(inp["L0_ffn_W1"], dtype=f32)
    w3 = np.ascontiguousarray(inp["L0_ffn_W3"], dtype=f32)
    w2 = np.ascontiguousarray(inp["L0_ffn_W2"], dtype=f32)
    ws1 = np.ascontiguousarray(inp["L0_ffn_Ws1"], dtype=f32)
    ws3 = np.ascontiguousarray(inp["L0_ffn_Ws3"], dtype=f32)
    ws2 = np.ascontiguousarray(inp["L0_ffn_Ws2"], dtype=f32)

    def qkv_tile(w):
        return np.ascontiguousarray(
            w.T.reshape(DC, 128, DC, 128).transpose(1, 0, 2, 3))

    d = {}
    d["wq_t"] = qkv_tile(wq)
    d["wk_t"] = qkv_tile(wk)
    d["wv_t"] = qkv_tile(wv)
    d["wo_t"] = np.ascontiguousarray(wo.T.reshape(DC, 128, D))
    d["wg_t"] = np.ascontiguousarray(wg.T.reshape(DC, 128, E).transpose(1, 0, 2))
    d["w1_t"] = np.ascontiguousarray(
        w1.transpose(0, 2, 1).reshape(E, DC, 128, ICN, 128).transpose(0, 3, 2, 1, 4))
    d["w3_t"] = np.ascontiguousarray(
        w3.transpose(0, 2, 1).reshape(E, DC, 128, ICN, 128).transpose(0, 3, 2, 1, 4))
    d["w2_t"] = np.ascontiguousarray(
        w2.transpose(0, 2, 1).reshape(E, ICN, 128, D))
    d["ws1_t"] = np.ascontiguousarray(
        ws1.T.reshape(DC, 128, NSH * ICN, 128).transpose(2, 1, 0, 3)
        .reshape(NSH, ICN, 128, DC, 128))
    d["ws3_t"] = np.ascontiguousarray(
        ws3.T.reshape(DC, 128, NSH * ICN, 128).transpose(2, 1, 0, 3)
        .reshape(NSH, ICN, 128, DC, 128))
    d["ws2_t"] = np.ascontiguousarray(
        ws2.T.reshape(NSH, ICN, 128, D))
    return d


def kernel(**inputs):
    x = np.ascontiguousarray(np.asarray(inputs["L0_attn_x"], dtype=np.float32))
    x2d = x.reshape(T, D)
    wts = _prep_weights(inputs)

    ident = np.eye(128, dtype=np.float32).astype(ml_dtypes.bfloat16)

    c = build_nc()
    nc = build_program(c)

    xtiles = x2d.reshape(NT, 128, D)
    in_maps = []
    for core in range(NCORE):
        mask = np.zeros((128, 8 * 128), dtype=np.float32)
        for r in range(8):
            if r < core:
                pass
            elif r == core:
                blk = np.where(np.triu(np.ones((128, 128))) > 0, 0.0, -1e30)
                mask[:, r * 128:(r + 1) * 128] = blk.T
            else:
                mask[:, r * 128:(r + 1) * 128] = -1e30
        m = {
            "x_shard": np.ascontiguousarray(xtiles[core::8].reshape(TL, D)),
            "x_full": x2d,
            "mask8": mask.astype(ml_dtypes.bfloat16),
            "ident_in": ident,
        }
        m.update(wts)
        in_maps.append(m)

    res = run_bass_kernel_spmd(nc, in_maps, core_ids=list(range(NCORE)))
    global LAST_RES
    LAST_RES = res

    out = np.empty((NT, 128, D), dtype=np.float32)
    for core in range(NCORE):
        out[core::8] = res.results[core]["out"].reshape(ML, 128, D)
    return out.reshape(1, T, D)


# version 08 (final)
# speedup vs baseline: 1.0476x; 1.0476x over previous
import contextlib
import math

import numpy as np
import ml_dtypes

import concourse.bass as bass
import concourse.mybir as mybir
from concourse.bass_utils import run_bass_kernel_spmd
from concourse.alu_op_type import AluOpType as OP

F32 = mybir.dt.float32
F32R = mybir.dt.float32r
BF16 = mybir.dt.bfloat16
AF = mybir.ActivationFunctionType
AX = mybir.AxisListType

DEBUG = False
PHASES = "ABCDEF"
LAST_RES = None

NCORE = 8
T = 8192
D = 2048
NH = 16
HD = 128
DC = 16
NT = 64
ML = 8
TL = ML * 128
E = 16
ICN = 11
NSH = 2
EPS = 1e-5
CSH = 16.0
ISQ = 1.0 / math.sqrt(HD)

OFF_IDENT = 0
OFF_ONES = 256
OFF_ONESROW = 288
OFF_ONESROW_F = 544
OFF_MASK = 1056
OFF_STAT = 3104
OFF_COMB = 3360
OFF_GSC = 3872
OFF_WG = 4256
OFF_SCR = 4768
DYN = 8896
OFF_LROW = DYN + 184320
OFF_LROW_FR = DYN + 188416

OFF_WK = DYN
OFF_WV = DYN + 65536
OFF_XRAW = DYN + 131072
OFF_XLNT = DYN + 163840
OFF_XLNF = DYN + 167936
OFF_KVST = DYN + 184320
OFF_WQ = DYN
OFF_XLNOF = DYN + 65536
OFF_QFM = DYN + 98304
OFF_XRAWB = DYN + 131072
OFF_XLNTB = DYN + 163840
OFF_KBUF = DYN
OFF_VTM = DYN + 32768
OFF_PTB = DYN + 147456
OFF_ZFM = DYN + 151552
OFF_WO = DYN
OFF_AOUT = DYN + 65536
OFF_X2T = DYN + 73728
OFF_XRAWD = DYN + 81920
OFF_XTT = DYN + 90112
OFF_XTF = DYN + 98304
OFF_H = DYN
OFF_W1B = DYN + 22528
OFF_W3B = DYN + 30720
OFF_W2A = DYN + 38912
OFF_Y = DYN + 131072
OFF_X2F = DYN + 8192 - 8192
OFF_FOUT = DYN + 8192

SBUF_BASE = 16512
SBUF_END = SBUF_BASE + DYN + 196608
assert SBUF_END <= 229247, SBUF_END


def _sb(nc, name, shape, dtype, off):
    return nc.alloc_sbuf_tensor_at(name, shape, dtype, offset=SBUF_BASE + off)


class Ctx:
    pass


def build_nc():
    nc = bass.Bass("TRN2", target_bir_lowering=False)
    c = Ctx()
    c.nc = nc

    din = lambda n, s, dt=F32: nc.dram_tensor(n, s, dt, kind="ExternalInput").ap()
    c.x_shard = din("x_shard", [TL, D])
    c.x_full = din("x_full", [T, D])
    c.mask8 = din("mask8", [128, 8 * 128], BF16)
    c.ident_in = din("ident_in", [128, 128], BF16)
    c.wq_t = din("wq_t", [128, DC, DC, 128])
    c.wk_t = din("wk_t", [128, DC, DC, 128])
    c.wv_t = din("wv_t", [128, DC, DC, 128])
    c.wo_t = din("wo_t", [DC, 128, D])
    c.wg_t = din("wg_t", [128, DC, E])
    c.w1_t = din("w1_t", [E, ICN, 128, DC, 128])
    c.w3_t = din("w3_t", [E, ICN, 128, DC, 128])
    c.w2_t = din("w2_t", [E, ICN, 128, D])
    c.ws1_t = din("ws1_t", [NSH, ICN, 128, DC, 128])
    c.ws3_t = din("ws3_t", [NSH, ICN, 128, DC, 128])
    c.ws2_t = din("ws2_t", [NSH, ICN, 128, D])
    c.out = nc.dram_tensor("out", [TL, D], F32, kind="ExternalOutput").ap()
    c.k_sc = nc.dram_tensor("k_sc", [D, T], BF16, kind="Internal").ap()
    c.v_sc = nc.dram_tensor("v_sc", [4, T, 4 * HD], BF16, kind="Internal").ap()
    c.x2_sc = nc.dram_tensor("x2_sc", [TL, D], F32, kind="Internal").ap()
    if DEBUG:
        dout = lambda n, s, dt: nc.dram_tensor(n, s, dt, kind="ExternalOutput").ap()
        c.dbg_k = dout("dbg_k", [D, T], BF16)
        c.dbg_v = dout("dbg_v", [4, T, 4 * HD], BF16)
        c.dbg_q = dout("dbg_q", [128, NH * 1024], BF16)
        c.dbg_z = dout("dbg_z", [128, NH * 1024], BF16)
        c.dbg_comb = dout("dbg_comb", [128, 128], F32)
        c.dbg_xtf = dout("dbg_xtf", [128, DC * 1024], BF16)
        c.dbg_x2 = dout("dbg_x2", [TL, D], F32)
        c.dbg_y = dout("dbg_y", [128, ML * 2048], F32)
        c.dbg_gsc = dout("dbg_gsc", [128, 96], F32)
        c.dbg_wk = dout("dbg_wk", [128, 2048], BF16)
        c.dbg_xlnf = dout("dbg_xlnf", [128, DC * 512], BF16)
        c.dbg_stat = dout("dbg_stat", [128, 64], F32)
        c.dbg_xraw = dout("dbg_xraw", [128, 4 * 2048], F32)
        c.dbg_vtm = dout("dbg_vtm", [128, 8192], BF16)
        c.dbg_lrow = dout("dbg_lrow", [1, 1024], F32)
        c.dbg_rscr = dout("dbg_rscr", [128, 1024], F32)

    c.ident = _sb(nc, "ident", [128, 128], BF16, OFF_IDENT)
    c.ones = _sb(nc, "ones", [128, 1], BF16, OFF_ONES)
    c.onesrow = _sb(nc, "onesrow", [1, 128], BF16, OFF_ONESROW)
    c.onesrow_f = _sb(nc, "onesrow_f", [1, 128], F32, OFF_ONESROW_F)
    c.mask = _sb(nc, "mask", [128, 8 * 128], BF16, OFF_MASK)
    c.stat = _sb(nc, "stat", [128, 64], F32, OFF_STAT)
    c.comb = _sb(nc, "comb", [128, 128], F32, OFF_COMB)
    c.gsc = _sb(nc, "gsc", [128, 96], F32, OFF_GSC)
    c.gsc8 = _sb(nc, "gsc8", [128, 8 * 96], F32, 65536 + 8896)
    c.wg = _sb(nc, "wg", [128, DC * E], BF16, OFF_WG)
    c.lrow = _sb(nc, "lrow", [1, 1024], F32, OFF_LROW)
    c.lrow2 = _sb(nc, "lrow2", [1, 1024], F32, OFF_LROW_FR)
    c.rscr = _sb(nc, "rscr", [128, 1024], F32, OFF_SCR)
    c.sqs = _sb(nc, "sqs", [128, 1024], F32, OFF_SCR)
    c.sil = _sb(nc, "sil", [128, 1024], F32, OFF_SCR)

    c.wk = _sb(nc, "wk", [128, DC * DC * 128], BF16, OFF_WK)
    c.wv = _sb(nc, "wv", [128, DC * DC * 128], BF16, OFF_WV)
    c.xraw = _sb(nc, "xraw", [128, 4 * 2048], F32, OFF_XRAW)
    c.xlnt = _sb(nc, "xlnt", [128, 2048], BF16, OFF_XLNT)
    c.xlnf = _sb(nc, "xlnf", [128, DC * 512], BF16, OFF_XLNF)
    c.kvst = _sb(nc, "kvst", [128, 4 * 512], BF16, OFF_KVST)

    c.wq = _sb(nc, "wq", [128, DC * DC * 128], BF16, OFF_WQ)
    c.xlnof = _sb(nc, "xlnof", [128, DC * 1024], BF16, OFF_XLNOF)
    c.qfm = _sb(nc, "qfm", [128, NH * 1024], BF16, OFF_QFM)
    c.xrawb = _sb(nc, "xrawb", [128, 2048], F32, OFF_XRAWB)
    c.xlntb = _sb(nc, "xlntb", [128, 2048], BF16, OFF_XLNTB)

    c.kbuf = _sb(nc, "kbuf", [128, 2 * 8192], BF16, OFF_KBUF)
    c.vtm = _sb(nc, "vtm", [128, 2 * 8192], BF16, OFF_VTM)
    c.ptb = _sb(nc, "ptb", [128, 2 * 1024], BF16, OFF_PTB)
    c.zfm = _sb(nc, "zfm", [128, NH * 1024], BF16, OFF_ZFM)

    c.wo = _sb(nc, "wo", [128, DC * 2048], BF16, OFF_WO)
    c.aout = _sb(nc, "aout", [128, 2048], F32, OFF_AOUT)
    c.x2t = _sb(nc, "x2t", [128, 2048], F32, OFF_X2T)
    c.xrawd = _sb(nc, "xrawd", [128, 2048], F32, OFF_XRAWD)
    c.xtt = _sb(nc, "xtt", [128, 2048], BF16, OFF_XTT)
    c.xtf = _sb(nc, "xtf", [128, DC * 1024], BF16, OFF_XTF)

    c.h = _sb(nc, "h", [128, ICN * 1024], BF16, OFF_H)
    c.w1b = _sb(nc, "w1b", [128, 2 * 2048], BF16, OFF_W1B)
    c.w3b = _sb(nc, "w3b", [128, 2 * 2048], BF16, OFF_W3B)
    c.w2a = _sb(nc, "w2a", [128, ICN * 2048], BF16, OFF_W2A)
    c.y = _sb(nc, "y", [128, ML * 2048], F32, OFF_Y)

    c.x2f = _sb(nc, "x2f", [128, 2048], F32, DYN)
    c.fout = _sb(nc, "fout", [128, 2048], F32, DYN + 8192)

    c.ST_MEAN = 0
    c.ST_VAR = 1
    c.ST_RSTD = 2
    c.ST_LNV = 3
    c.ST_EPS = 4
    c.ST_NEGC = 5
    c.ST_NMAX = 6
    c.ST_SEXP = 7
    c.ST_RSE = 8
    c.ST_MEAN2 = 12
    c.ST_VAR2 = 13

    return c


def build_program(c):
    nc = c.nc
    stack = contextlib.ExitStack()
    with stack:
        sem = lambda n: stack.enter_context(nc.semaphore(n))
        c.sem_sq = sem("sq")
        c.sem_gq = sem("gq")
        c.sem_xq = sem("xq")
        c.sem_tm = sem("tm")
        c.sem_vm = sem("vm")
        c.sem_am = sem("am")
        c.sem_ph = sem("ph")
        pbA = stack.enter_context(nc.psum_tensor("pbA", [128, 1024], F32))
        pbB = stack.enter_context(nc.psum_tensor("pbB", [128, 1024], F32))
        pbC = stack.enter_context(nc.psum_tensor("pbC", [128, 1024], F32))
        pbD = stack.enter_context(nc.psum_tensor("pbD", [128, 1024], F32))
        block = stack.enter_context(nc.Block())

        sq = [0]
        gq = [0]
        xq = [0]
        tm = [0]
        vm = [0]
        am = [0]
        ph = [0]

        def barrier():
            ph[0] += 5
            tgt = ph[0]

            @block.sync
            def _(s):
                s.drain()
                s.sem_inc(c.sem_ph, 1)
                s.wait_ge(c.sem_ph, tgt)

            @block.tensor
            def _(t):
                t.drain()
                t.sem_inc(c.sem_ph, 1)
                t.wait_ge(c.sem_ph, tgt)

            @block.vector
            def _(v):
                v.drain()
                v.sem_inc(c.sem_ph, 1)
                v.wait_ge(c.sem_ph, tgt)

            @block.scalar
            def _(a):
                a.drain()
                a.sem_inc(c.sem_ph, 1)
                a.wait_ge(c.sem_ph, tgt)

            @block.gpsimd
            def _(g):
                g.drain()
                g.sem_inc(c.sem_ph, 1)
                g.wait_ge(c.sem_ph, tgt)

        @block.gpsimd
        def _(g):
            g.dma_start(out=c.wk[:], in_=c.wk_t.rearrange("p a b o -> p (a b o)")
                        ).then_inc(c.sem_gq, 16)
            g.dma_start(out=c.wv[:], in_=c.wv_t.rearrange("p a b o -> p (a b o)")
                        ).then_inc(c.sem_gq, 16)
            g.dma_start(out=c.wg[:], in_=c.wg_t.rearrange("p a e -> p (a e)")
                        ).then_inc(c.sem_gq, 16)
        gq[0] += 48
        GQ_PRE = gq[0]

        @block.sync
        def _(s):
            s.dma_start(out=c.mask[:], in_=c.mask8).then_inc(c.sem_sq, 16)
            s.dma_start(out=c.ident[:], in_=c.ident_in).then_inc(c.sem_sq, 16)
        sq[0] += 32
        SQ_PRE = sq[0]

        @block.vector
        def _(v):
            v.memset(c.ones[:], 1.0)
            v.memset(c.onesrow[:], 1.0)
            v.memset(c.stat[:, c.ST_EPS:c.ST_EPS + 1], EPS)
            v.memset(c.stat[:, c.ST_NEGC:c.ST_NEGC + 1], -CSH)
            v.memset(c.stat[:, 44:48], 0.0).then_inc(c.sem_vm, 1)
        vm[0] += 1

        x_t = c.x_full.rearrange("(s j p) d -> s p j d", p=128, j=4)

        A_TM_PER_SWEEP = 64 + 32
        A_VM_PER_SWEEP = 4 * (2 + 16) + 32
        A_AM_PER_SWEEP = 4
        A_TM0, A_VM0, A_AM0, A_SQ0, A_GQ0 = tm[0], vm[0], am[0], sq[0], gq[0]
        A_XQ0 = xq[0]

        @block.sync
        def _(s):
            for sw in range(16):
                for oc in range(DC):
                    s.wait_ge(c.sem_vm,
                              A_VM0 + sw * A_VM_PER_SWEEP + 72 + oc + 1)
                    slot = oc % 2
                    s.dma_start(
                        out=c.k_sc[oc * 128:(oc + 1) * 128, sw * 512:(sw + 1) * 512],
                        in_=c.kvst[:, slot * 512:(slot + 1) * 512],
                    ).then_inc(c.sem_sq, 16)
                for g in range(16):
                    t4, oc4 = divmod(g, 4)
                    s.wait_ge(c.sem_vm,
                              A_VM0 + sw * A_VM_PER_SWEEP + 88 + g + 1)
                    slot = 2 + g % 2
                    s.dma_start(
                        out=c.v_sc[oc4,
                                   sw * 512 + t4 * 128: sw * 512 + (t4 + 1) * 128,
                                   :],
                        in_=c.kvst[:, slot * 512:(slot + 1) * 512],
                    ).then_inc(c.sem_sq, 16)
        sq[0] += 16 * 32 * 16

        @block.scalar
        def _(a):
            for sw in range(16):
                if sw >= 1:
                    a.wait_ge(c.sem_tm, A_TM0 + (sw - 1) * A_TM_PER_SWEEP + 64)
                a.dma_start(out=c.xraw[:].rearrange("p (j d) -> p j d", j=4),
                            in_=x_t[sw]).then_inc(c.sem_xq, 16)
                for j in range(4):
                    a.wait_ge(c.sem_vm, A_VM0 + sw * A_VM_PER_SWEEP + j * 18 + 1)
                    a.activation(c.stat[:, c.ST_LNV:c.ST_LNV + 1],
                                 c.stat[:, c.ST_VAR2:c.ST_VAR2 + 1],
                                 AF.Ln, bias=c.stat[:, c.ST_EPS:c.ST_EPS + 1])
                    a.activation(c.stat[:, 45:46], c.stat[:, c.ST_EPS:c.ST_EPS + 1],
                                 AF.Square)
                    a.activation(c.stat[:, c.ST_RSTD:c.ST_RSTD + 1],
                                 c.stat[:, c.ST_LNV:c.ST_LNV + 1],
                                 AF.Exp, scale=-0.5).then_inc(c.sem_am, 1)
        am[0] += 16 * A_AM_PER_SWEEP
        xq[0] += 16 * 16

        @block.vector
        def _(v):
            st = c.stat
            for sw in range(16):
                v.wait_ge(c.sem_xq, A_XQ0 + (sw + 1) * 16)
                for j in range(4):
                    src = c.xraw[:, j * 2048:(j + 1) * 2048]
                    for q in range(4):
                        v.bn_stats(st[:, 16 + q * 6:22 + q * 6],
                                   src[:, q * 512:(q + 1) * 512])
                    v.tensor_copy(c.rscr[:, 0:512], c.rscr[:, 512:1024])
                    v.bn_aggr(st[:, 0:2], st[:, 16:40])
                    v.tensor_copy(c.rscr[:, 0:512], c.rscr[:, 512:1024])
                    v.tensor_copy(st[:, 12:14], st[:, 0:2]).then_inc(c.sem_vm, 1)
                    v.wait_ge(c.sem_am, A_AM0 + sw * A_AM_PER_SWEEP + j + 1)
                    v.tensor_scalar(c.xlnt[:], src,
                                    st[:, c.ST_MEAN2:c.ST_MEAN2 + 1],
                                    st[:, c.ST_RSTD:c.ST_RSTD + 1],
                                    OP.subtract, OP.mult).then_inc(c.sem_vm, 1)
                    for dcx in range(DC):
                        tr_i = j * 16 + dcx
                        v.wait_ge(c.sem_tm, A_TM0 + sw * A_TM_PER_SWEEP + tr_i + 1)
                        slotv = pbC[:].bitcast(BF16)
                        srcv = (slotv[:, 0:128] if tr_i % 2 == 0
                                else slotv[:, 1024:1152])
                        v.tensor_copy(
                            c.xlnf[:, dcx * 512 + j * 128: dcx * 512 + (j + 1) * 128],
                            srcv).then_inc(c.sem_vm, 1)
                for oc in range(DC):
                    v.wait_ge(c.sem_tm,
                              A_TM0 + sw * A_TM_PER_SWEEP + 64 + oc + 1)
                    half = oc % 2
                    if sw >= 1 or oc >= 2:
                        if oc >= 2:
                            prev_store = sw * 32 + oc - 1
                        else:
                            prev_store = (sw - 1) * 32 + 15 + oc
                        v.wait_ge(c.sem_sq, A_SQ0 + prev_store * 16)
                    v.tensor_copy(
                        c.kvst[:, half * 512:(half + 1) * 512],
                        pbA[:, half * 512:(half + 1) * 512]).then_inc(c.sem_vm, 1)
                for g in range(16):
                    v.wait_ge(c.sem_tm,
                              A_TM0 + sw * A_TM_PER_SWEEP + 80 + g + 1)
                    half = g % 2
                    if sw >= 1 or g >= 2:
                        if g >= 2:
                            prev_store = sw * 32 + 15 + g
                        else:
                            prev_store = (sw - 1) * 32 + 31 + g
                        v.wait_ge(c.sem_sq, A_SQ0 + prev_store * 16)
                    v.tensor_copy(
                        c.kvst[:, (2 + half) * 512:(3 + half) * 512],
                        pbB[:, half * 512:(half + 1) * 512]).then_inc(c.sem_vm, 1)
        vm[0] += 16 * A_VM_PER_SWEEP

        @block.tensor
        def _(t):
            t.wait_ge(c.sem_sq, SQ_PRE)
            for sw in range(16):
                for j in range(4):
                    t.wait_ge(c.sem_vm, A_VM0 + sw * A_VM_PER_SWEEP + j * 18 + 2)
                    for dcx in range(DC):
                        tr_i = j * 16 + dcx
                        if tr_i >= 2:
                            t.wait_ge(c.sem_vm, A_VM0 + sw * A_VM_PER_SWEEP
                                      + _a_evac_vm(j * 16 + dcx - 2))
                        elif sw >= 1:
                            t.wait_ge(c.sem_vm, A_VM0 + (sw - 1) * A_VM_PER_SWEEP
                                      + _a_evac_vm(62 + tr_i))
                        slotv = pbC[:].bitcast(BF16)
                        dstv = (slotv[:, 0:128] if tr_i % 2 == 0
                                else slotv[:, 1024:1152])
                        t.transpose(dstv, c.xlnt[:, dcx * 128:(dcx + 1) * 128],
                                    c.ident[:]).then_inc(c.sem_tm, 1)
                if sw == 0:
                    t.wait_ge(c.sem_gq, GQ_PRE)
                t.wait_ge(c.sem_vm, A_VM0 + sw * A_VM_PER_SWEEP + 72)
                wk4 = c.wk[:].rearrange("p (a b o) -> p a b o", a=DC, b=DC)
                wv4 = c.wv[:].rearrange("p (a b o) -> p a b o", a=DC, b=DC)
                for oc in range(DC):
                    half = oc % 2
                    if oc >= 2:
                        t.wait_ge(c.sem_vm, A_VM0 + sw * A_VM_PER_SWEEP
                                  + 72 + oc - 2 + 1)
                    elif sw >= 1:
                        t.wait_ge(c.sem_vm, A_VM0 + (sw - 1) * A_VM_PER_SWEEP
                                  + 72 + 14 + oc + 1)
                    for dcx in range(DC):
                        ins = t.matmul(pbA[:, half * 512:(half + 1) * 512],
                                       wk4[:, dcx, oc, :],
                                       c.xlnf[:, dcx * 512:(dcx + 1) * 512],
                                       start=(dcx == 0), stop=(dcx == DC - 1))
                    ins.then_inc(c.sem_tm, 1)
                for g in range(16):
                    t4, oc4 = divmod(g, 4)
                    half = g % 2
                    if g >= 2:
                        t.wait_ge(c.sem_vm, A_VM0 + sw * A_VM_PER_SWEEP
                                  + 88 + g - 2 + 1)
                    elif sw >= 1:
                        t.wait_ge(c.sem_vm, A_VM0 + (sw - 1) * A_VM_PER_SWEEP
                                  + 88 + 14 + g + 1)
                    for dcx in range(DC):
                        ins = t.matmul(
                            pbB[:, half * 512:(half + 1) * 512],
                            c.xlnf[:, dcx * 512 + t4 * 128:
                                   dcx * 512 + (t4 + 1) * 128],
                            wv4[:, dcx, oc4 * 4:(oc4 + 1) * 4, :],
                            start=(dcx == 0), stop=(dcx == DC - 1))
                    ins.then_inc(c.sem_tm, 1)
        tm[0] += 16 * A_TM_PER_SWEEP

        if DEBUG:
            @block.sync
            def _(s):
                s.wait_ge(c.sem_vm, vm[0])
                s.dma_start(out=c.dbg_k, in_=c.k_sc).then_inc(c.sem_sq, 16)
                s.dma_start(out=c.dbg_v, in_=c.v_sc).then_inc(c.sem_sq, 16)
                s.dma_start(out=c.dbg_wk, in_=c.wk[:, 0:2048]).then_inc(c.sem_sq, 16)
                s.dma_start(out=c.dbg_xlnf, in_=c.xlnf[:]).then_inc(c.sem_sq, 16)
                s.dma_start(out=c.dbg_stat, in_=c.stat[:]).then_inc(c.sem_sq, 16)
                s.dma_start(out=c.dbg_xraw, in_=c.xraw[:]).then_inc(c.sem_sq, 16)
            sq[0] += 96

        @block.sync
        def _(s):
            s.wait_ge(c.sem_sq, sq[0])

        @block.gpsimd
        def _(g):
            g.wait_ge(c.sem_gq, gq[0])

        @block.scalar
        def _(a):
            a.wait_ge(c.sem_xq, xq[0])

        barrier()

        if "B" not in PHASES:
            return c.nc
        B_TM0, B_VM0, B_AM0, B_SQ0, B_GQ0 = tm[0], vm[0], am[0], sq[0], gq[0]
        B_VM_TILE = 2 + 16
        B_TM_TILE = 16

        @block.gpsimd
        def _(g):
            g.dma_start(out=c.wq[:], in_=c.wq_t.rearrange("p a b o -> p (a b o)")
                        ).then_inc(c.sem_gq, 16)
        gq[0] += 16

        @block.sync
        def _(s):
            for m in range(ML):
                if m >= 1:
                    s.wait_ge(c.sem_tm, B_TM0 + m * B_TM_TILE)
                s.dma_start(out=c.xrawb[:],
                            in_=c.x_shard[m * 128:(m + 1) * 128, :]
                            ).then_inc(c.sem_sq, 16)
        sq[0] += ML * 16

        @block.scalar
        def _(a):
            for m in range(ML):
                a.wait_ge(c.sem_vm, B_VM0 + m * B_VM_TILE + 1)
                a.activation(c.stat[:, c.ST_LNV:c.ST_LNV + 1],
                             c.stat[:, c.ST_VAR2:c.ST_VAR2 + 1],
                             AF.Ln, bias=c.stat[:, c.ST_EPS:c.ST_EPS + 1])
                a.activation(c.stat[:, 45:46], c.stat[:, c.ST_EPS:c.ST_EPS + 1],
                             AF.Square)
                a.activation(c.stat[:, c.ST_RSTD:c.ST_RSTD + 1],
                             c.stat[:, c.ST_LNV:c.ST_LNV + 1],
                             AF.Exp, scale=-0.5).then_inc(c.sem_am, 1)
        am[0] += ML

        @block.vector
        def _(v):
            st = c.stat
            for m in range(ML):
                v.wait_ge(c.sem_sq, B_SQ0 + (m + 1) * 16)
                src = c.xrawb[:]
                for q in range(4):
                    v.bn_stats(st[:, 16 + q * 6:22 + q * 6],
                               src[:, q * 512:(q + 1) * 512])
                v.tensor_copy(c.rscr[:, 0:512], c.rscr[:, 512:1024])
                v.bn_aggr(st[:, 0:2], st[:, 16:40])
                v.tensor_copy(c.rscr[:, 0:512], c.rscr[:, 512:1024])
                v.tensor_copy(st[:, 12:14], st[:, 0:2]).then_inc(c.sem_vm, 1)
                v.wait_ge(c.sem_am, B_AM0 + m + 1)
                v.tensor_scalar(c.xlntb[:], src,
                                st[:, c.ST_MEAN2:c.ST_MEAN2 + 1],
                                st[:, c.ST_RSTD:c.ST_RSTD + 1],
                                OP.subtract, OP.mult).then_inc(c.sem_vm, 1)
                for dcx in range(DC):
                    v.wait_ge(c.sem_tm, B_TM0 + m * B_TM_TILE + dcx + 1)
                    slotv = pbC[:].bitcast(BF16)
                    srcv = (slotv[:, 0:128] if dcx % 2 == 0
                            else slotv[:, 1024:1152])
                    v.tensor_copy(
                        c.xlnof[:, dcx * 1024 + m * 128: dcx * 1024 + (m + 1) * 128],
                        srcv).then_inc(c.sem_vm, 1)
            for oc in range(NH):
                pb = pbA if oc % 2 == 0 else pbB
                v.wait_ge(c.sem_tm, B_TM0 + ML * B_TM_TILE + oc + 1)
                v.tensor_copy(c.qfm[:, oc * 1024:oc * 1024 + 512],
                              pb[:, 0:512]).then_inc(c.sem_vm, 1)
                v.tensor_copy(c.qfm[:, oc * 1024 + 512:(oc + 1) * 1024],
                              pb[:, 512:1024]).then_inc(c.sem_vm, 1)
        vm[0] += ML * B_VM_TILE + NH * 2

        @block.tensor
        def _(t):
            for m in range(ML):
                t.wait_ge(c.sem_vm, B_VM0 + m * B_VM_TILE + 2)
                for dcx in range(DC):
                    tr_i = m * 16 + dcx
                    if tr_i >= 2:
                        pm, pd = divmod(tr_i - 2, 16)
                        t.wait_ge(c.sem_vm, B_VM0 + pm * B_VM_TILE + 2 + pd + 1)
                    slotv = pbC[:].bitcast(BF16)
                    dstv = (slotv[:, 0:128] if dcx % 2 == 0
                            else slotv[:, 1024:1152])
                    t.transpose(dstv, c.xlntb[:, dcx * 128:(dcx + 1) * 128],
                                c.ident[:]).then_inc(c.sem_tm, 1)
            wq4 = c.wq[:].rearrange("p (a b o) -> p a b o", a=DC, b=DC)
            t.wait_ge(c.sem_gq, gq[0])
            t.wait_ge(c.sem_vm, B_VM0 + ML * B_VM_TILE)
            for oc in range(NH):
                pb = pbA if oc % 2 == 0 else pbB
                if oc >= 2:
                    t.wait_ge(c.sem_vm, B_VM0 + ML * B_VM_TILE + (oc - 2) * 2 + 2)
                for dcx in range(DC):
                    for tch in range(2):
                        ins = t.matmul(pb[:, tch * 512:(tch + 1) * 512],
                                       wq4[:, dcx, oc, :],
                                       c.xlnof[:, dcx * 1024 + tch * 512:
                                               dcx * 1024 + (tch + 1) * 512],
                                       start=(dcx == 0), stop=(dcx == DC - 1))
                ins.then_inc(c.sem_tm, 1)
        tm[0] += ML * B_TM_TILE + NH

        if DEBUG:
            @block.sync
            def _(s):
                s.wait_ge(c.sem_vm, vm[0])
                s.dma_start(out=c.dbg_q, in_=c.qfm[:]).then_inc(c.sem_sq, 16)
            sq[0] += 16

        @block.sync
        def _(s):
            s.wait_ge(c.sem_sq, sq[0])

        @block.gpsimd
        def _(g):
            g.wait_ge(c.sem_gq, gq[0])

        @block.scalar
        def _(a):
            a.wait_ge(c.sem_xq, xq[0])

        barrier()

        if "C" not in PHASES:
            return c.nc
        C_TM0, C_VM0, C_AM0, C_SQ0 = tm[0], vm[0], am[0], sq[0]
        C_TM_H = 64 + 64 + 1
        C_VM_H = 2
        C_AM_H = 64

        def c_tm_st(h, jj):
            return C_TM0 + h * C_TM_H + (1 if jj == 0 else 2 * jj)

        def c_tm_lpv(h, jj):
            if jj == 63:
                return C_TM0 + h * C_TM_H + 128
            return C_TM0 + h * C_TM_H + 2 * jj + 3

        def c_tm_bcast(h):
            return C_TM0 + (h + 1) * C_TM_H

        def c_vm_lrec(h):
            return C_VM0 + h * C_VM_H + 1

        def c_vm_zev(h):
            return C_VM0 + (h + 1) * C_VM_H

        def c_am_exp(h, jj):
            return C_AM0 + h * C_AM_H + jj + 1

        def parts_of(jj):
            a0 = (jj // 8) * 128
            p = []
            if a0 < 512:
                p.append((a0, 512))
            p.append((max(a0, 512), 1024))
            return p

        @block.sync
        def _(s):
            for h in range(NH):
                if h >= 2:
                    s.wait_ge(c.sem_tm, c_tm_lpv(h - 2, 63))
                s.dma_start(out=c.kbuf[:, (h % 2) * 8192:(h % 2 + 1) * 8192],
                            in_=c.k_sc[h * 128:(h + 1) * 128, :]
                            ).then_inc(c.sem_sq, 16)
                for b8 in range(8):
                    s.dma_start(
                        out=c.vtm[:, (h % 2) * 8192 + b8 * 1024:
                                  (h % 2) * 8192 + (b8 + 1) * 1024]
                        .rearrange("p (jj d) -> p jj d", d=128),
                        in_=c.v_sc[h // 4, b8 * 1024:(b8 + 1) * 1024,
                                   (h % 4) * 128:(h % 4 + 1) * 128]
                        .rearrange("(jj p) d -> p jj d", p=128)
                    ).then_inc(c.sem_sq, 16)
        sq[0] += NH * 16 * 9

        @block.tensor
        def _(t):
            for h in range(NH):
                kb = c.kbuf[:, (h % 2) * 8192:(h % 2 + 1) * 8192]
                vt = c.vtm[:, (h % 2) * 8192:(h % 2 + 1) * 8192]
                t.wait_ge(c.sem_sq, C_SQ0 + 9 * (h + 1) * 16)
                if h >= 1:
                    t.wait_ge(c.sem_am, C_AM0 + h * C_AM_H)
                    t.wait_ge(c.sem_vm, c_vm_zev(h - 1))
                qh = c.qfm[:, h * 1024:(h + 1) * 1024]

                def emit_st(jj):
                    stp = pbC if jj % 2 == 0 else pbD
                    if jj >= 2:
                        t.wait_ge(c.sem_am, c_am_exp(h, jj - 2))
                    a0 = (jj // 8) * 128
                    for (p0, p1) in parts_of(jj):
                        first_chunk = p0 <= a0 < p1
                        t.matmul(stp[:, p0:p1],
                                 kb[:, jj * 128:(jj + 1) * 128],
                                 qh[:, p0:p1], start=True,
                                 stop=not first_chunk)
                    r = jj % 8
                    ins = t.matmul(stp[:, a0:a0 + 128],
                                   c.mask[:, r * 128:(r + 1) * 128],
                                   c.ident[:], start=False, stop=True)
                    ins.then_inc(c.sem_tm, 1)

                def emit_lpv(jj):
                    ptp = c.ptb[:, (jj % 2) * 1024:(jj % 2 + 1) * 1024]
                    t.wait_ge(c.sem_am, c_am_exp(h, jj))
                    first = (jj == 0)
                    last = (jj == 63)
                    for (p0, p1) in parts_of(jj):
                        t.matmul(pbB[0:1, p0:p1], c.ones[:], ptp[:, p0:p1],
                                 start=first, stop=last)
                    for (p0, p1) in parts_of(jj):
                        ins = t.matmul(pbA[:, p0:p1],
                                       vt[:, jj * 128:(jj + 1) * 128],
                                       ptp[:, p0:p1], start=first, stop=last)
                    ins.then_inc(c.sem_tm, 1)

                emit_st(0)
                for jj in range(1, NT):
                    emit_st(jj)
                    emit_lpv(jj - 1)
                emit_lpv(63)
                t.wait_ge(c.sem_vm, c_vm_lrec(h))
                t.wait_ge(c.sem_am, c_am_exp(h, 63))
                for tch in range(2):
                    ins = t.matmul(pbC[:, tch * 512:(tch + 1) * 512],
                                   c.onesrow_f[:],
                                   c.lrow2[0:1, tch * 512:(tch + 1) * 512],
                                   start=True, stop=True)
                ins.then_inc(c.sem_tm, 1)
        tm[0] += NH * C_TM_H

        @block.scalar
        def _(a):
            for h in range(NH):
                for jj in range(NT):
                    stp = pbC if jj % 2 == 0 else pbD
                    ptp = c.ptb[:, (jj % 2) * 1024:(jj % 2 + 1) * 1024]
                    a0 = (jj // 8) * 128
                    a.wait_ge(c.sem_tm, c_tm_st(h, jj))
                    if jj >= 2:
                        a.wait_ge(c.sem_tm, c_tm_lpv(h, jj - 2))
                    elif h >= 1:
                        a.wait_ge(c.sem_tm, c_tm_lpv(h - 1, 62 + jj))
                    a.activation(ptp[:, a0:1024], stp[:, a0:1024], AF.Exp,
                                 bias=c.stat[:, c.ST_NEGC:c.ST_NEGC + 1],
                                 scale=ISQ).then_inc(c.sem_am, 1)
        am[0] += NH * C_AM_H

        @block.vector
        def _(v):
            v.memset(c.onesrow_f[:], 1.0)
            for h in range(NH):
                v.wait_ge(c.sem_tm, c_tm_lpv(h, 63))
                v.tensor_copy(c.lrow[0:1, :], pbB[0:1, :])
                v.reciprocal(c.lrow[0:1, :], c.lrow[0:1, :])
                v.tensor_copy(c.rscr[:, 0:512], c.rscr[:, 512:1024])
                v.tensor_copy(c.lrow2[0:1, :], c.lrow[0:1, :]).then_inc(c.sem_vm, 1)
                v.wait_ge(c.sem_tm, c_tm_bcast(h))
                v.tensor_copy(c.rscr[:], pbC[:])
                v.tensor_mul(
                    c.zfm[:, h * 1024: h * 1024 + 512],
                    pbA[:, 0:512], c.rscr[:, 0:512])
                v.tensor_mul(
                    c.zfm[:, h * 1024 + 512: (h + 1) * 1024],
                    pbA[:, 512:1024], c.rscr[:, 512:1024]).then_inc(c.sem_vm, 1)
        vm[0] += NH * C_VM_H

        if DEBUG:
            @block.sync
            def _(s):
                s.wait_ge(c.sem_vm, vm[0])
                s.dma_start(out=c.dbg_z, in_=c.zfm[:]).then_inc(c.sem_sq, 16)
                s.dma_start(out=c.dbg_vtm, in_=c.vtm[:, 8192:16384]).then_inc(c.sem_sq, 16)
                s.dma_start(out=c.dbg_lrow, in_=c.lrow[0:1, :]).then_inc(c.sem_sq, 16)
                s.dma_start(out=c.dbg_rscr, in_=c.rscr[:]).then_inc(c.sem_sq, 16)
            sq[0] += 64

        @block.sync
        def _(s):
            s.wait_ge(c.sem_sq, sq[0])

        barrier()

        if "D" not in PHASES:
            return c.nc
        D_TM0, D_VM0, D_AM0, D_SQ0, D_GQ0 = tm[0], vm[0], am[0], sq[0], gq[0]

        @block.gpsimd
        def _(g):
            g.dma_start(out=c.wo[:].rearrange("p (a o) -> p a o", a=DC),
                        in_=c.wo_t.rearrange("a p o -> p a o")
                        ).then_inc(c.sem_gq, 16)
        gq[0] += 16

        D_TM_M = 4 + 16
        D_VM_M = 5 + 16
        D_AM_M = 2

        @block.sync
        def _(s):
            for m in range(ML):
                s.dma_start(out=c.xrawd[:],
                            in_=c.x_shard[m * 128:(m + 1) * 128, :]
                            ).then_inc(c.sem_sq, 16)
                s.wait_ge(c.sem_vm, D_VM0 + m * D_VM_M + 3)
                s.dma_start(out=c.x2_sc[m * 128:(m + 1) * 128, :],
                            in_=c.x2t[:]).then_inc(c.sem_sq, 16)
        sq[0] += ML * 32

        @block.tensor
        def _(t):
            t.wait_ge(c.sem_gq, gq[0])
            wo3 = c.wo[:].rearrange("p (a o) -> p a o", a=DC)
            for m in range(ML):
                for nc4 in range(4):
                    pb = (pbA, pbA, pbB, pbB)[nc4]
                    half = nc4 % 2
                    if m >= 1:
                        t.wait_ge(c.sem_vm, D_VM0 + (m - 1) * D_VM_M + 1)
                    for zc in range(DC):
                        ins = t.matmul(pb[:, half * 512:(half + 1) * 512],
                                       c.zfm[:, zc * 1024 + m * 128:
                                             zc * 1024 + (m + 1) * 128],
                                       wo3[:, zc, nc4 * 512:(nc4 + 1) * 512],
                                       start=(zc == 0), stop=(zc == DC - 1))
                    ins.then_inc(c.sem_tm, 1)
                t.wait_ge(c.sem_vm, D_VM0 + m * D_VM_M + 5)
                for dcx in range(DC):
                    tr_i = m * 16 + dcx
                    if tr_i >= 2:
                        pm, pd = divmod(tr_i - 2, 16)
                        t.wait_ge(c.sem_vm, D_VM0 + pm * D_VM_M + 5 + pd + 1)
                    slotv = pbD[:].bitcast(BF16)
                    dstv = (slotv[:, 0:128] if dcx % 2 == 0
                            else slotv[:, 1024:1152])
                    t.transpose(dstv, c.xtt[:, dcx * 128:(dcx + 1) * 128],
                                c.ident[:]).then_inc(c.sem_tm, 1)
            wg3 = c.wg[:].rearrange("p (a e) -> p a e", a=DC)
            t.wait_ge(c.sem_vm, D_VM0 + ML * D_VM_M)
            for m in range(ML):
                for dcx in range(DC):
                    ins = t.matmul(pbC[:, m * 16:(m + 1) * 16],
                                   c.xtf[:, dcx * 1024 + m * 128:
                                         dcx * 1024 + (m + 1) * 128],
                                   wg3[:, dcx, :],
                                   start=(dcx == 0), stop=(dcx == DC - 1))
            ins.then_inc(c.sem_tm, 1)
        tm[0] += ML * D_TM_M + 1

        @block.scalar
        def _(a):
            for m in range(ML):
                a.wait_ge(c.sem_vm, D_VM0 + m * D_VM_M + 2)
                a.activation(c.stat[:, c.ST_LNV:c.ST_LNV + 1],
                             c.stat[:, c.ST_VAR2:c.ST_VAR2 + 1],
                             AF.Ln, bias=c.stat[:, c.ST_EPS:c.ST_EPS + 1])
                a.activation(c.stat[:, 45:46], c.stat[:, c.ST_EPS:c.ST_EPS + 1],
                             AF.Square)
                a.activation(c.stat[:, c.ST_RSTD:c.ST_RSTD + 1],
                             c.stat[:, c.ST_LNV:c.ST_LNV + 1],
                             AF.Exp, scale=-0.5).then_inc(c.sem_am, 1)
                a.wait_ge(c.sem_vm, D_VM0 + m * D_VM_M + 4)
                a.activation(c.stat[:, c.ST_LNV:c.ST_LNV + 1],
                             c.stat[:, c.ST_VAR2:c.ST_VAR2 + 1],
                             AF.Ln, bias=c.stat[:, c.ST_EPS:c.ST_EPS + 1])
                a.activation(c.stat[:, 45:46], c.stat[:, c.ST_EPS:c.ST_EPS + 1],
                             AF.Square)
                a.activation(c.stat[:, c.ST_RSTD:c.ST_RSTD + 1],
                             c.stat[:, c.ST_LNV:c.ST_LNV + 1],
                             AF.Exp, scale=-0.5).then_inc(c.sem_am, 1)
            a.wait_ge(c.sem_vm, D_VM0 + ML * D_VM_M + 1)
            for m in range(ML):
                g8 = c.gsc8[:, m * 96:(m + 1) * 96]
                ins = a.activation(g8[:, 0:16], pbC[:, m * 16:(m + 1) * 16],
                                   AF.Exp, bias=g8[:, 84:85])
            ins.then_inc(c.sem_am, 1)
        am[0] += ML * D_AM_M + 1

        @block.vector
        def _(v):
            st = c.stat
            for m in range(ML):
                for nc4 in range(4):
                    pb = (pbA, pbA, pbB, pbB)[nc4]
                    half = nc4 % 2
                    v.wait_ge(c.sem_tm, D_TM0 + m * D_TM_M + nc4 + 1)
                    ins = v.tensor_copy(c.aout[:, nc4 * 512:(nc4 + 1) * 512],
                                        pb[:, half * 512:(half + 1) * 512])
                ins.then_inc(c.sem_vm, 1)
                for q in range(4):
                    v.bn_stats(st[:, 16 + q * 6:22 + q * 6],
                               c.aout[:, q * 512:(q + 1) * 512])
                v.tensor_copy(c.rscr[:, 0:512], c.rscr[:, 512:1024])
                v.bn_aggr(st[:, 0:2], st[:, 16:40])
                v.tensor_copy(c.rscr[:, 0:512], c.rscr[:, 512:1024])
                v.tensor_copy(st[:, 12:14], st[:, 0:2]).then_inc(c.sem_vm, 1)
                v.wait_ge(c.sem_am, D_AM0 + m * D_AM_M + 1)
                v.tensor_scalar(c.x2t[:], c.aout[:],
                                st[:, c.ST_MEAN2:c.ST_MEAN2 + 1],
                                st[:, c.ST_RSTD:c.ST_RSTD + 1],
                                OP.subtract, OP.mult)
                v.wait_ge(c.sem_sq, D_SQ0 + (m * 2 + 1) * 16)
                v.tensor_add(c.x2t[:], c.x2t[:], c.xrawd[:]).then_inc(c.sem_vm, 1)
                for q in range(4):
                    v.bn_stats(st[:, 16 + q * 6:22 + q * 6],
                               c.x2t[:, q * 512:(q + 1) * 512])
                v.tensor_copy(c.rscr[:, 0:512], c.rscr[:, 512:1024])
                v.bn_aggr(st[:, 0:2], st[:, 16:40])
                v.tensor_copy(c.rscr[:, 0:512], c.rscr[:, 512:1024])
                v.tensor_copy(st[:, 12:14], st[:, 0:2]).then_inc(c.sem_vm, 1)
                v.wait_ge(c.sem_am, D_AM0 + m * D_AM_M + 2)
                v.tensor_scalar(c.xtt[:], c.x2t[:],
                                st[:, c.ST_MEAN2:c.ST_MEAN2 + 1],
                                st[:, c.ST_RSTD:c.ST_RSTD + 1],
                                OP.subtract, OP.mult).then_inc(c.sem_vm, 1)
                for dcx in range(DC):
                    v.wait_ge(c.sem_tm, D_TM0 + m * D_TM_M + 4 + dcx + 1)
                    slotv = pbD[:].bitcast(BF16)
                    srcv = (slotv[:, 0:128] if dcx % 2 == 0
                            else slotv[:, 1024:1152])
                    v.tensor_copy(
                        c.xtf[:, dcx * 1024 + m * 128: dcx * 1024 + (m + 1) * 128],
                        srcv).then_inc(c.sem_vm, 1)
            G = lambda m: c.gsc8[:, m * 96:(m + 1) * 96]
            v.wait_ge(c.sem_tm, D_TM0 + ML * D_TM_M + 1)
            for m in range(ML):
                ins = v.reduce_max(G(m)[:, 84:85], pbC[:, m * 16:(m + 1) * 16],
                                   axis=AX.X, negate=True)
            ins.then_inc(c.sem_vm, 1)
            v.wait_ge(c.sem_am, D_AM0 + ML * D_AM_M + 1)
            for m in range(ML):
                v.reduce_sum(G(m)[:, 85:86], G(m)[:, 0:16], axis=AX.X)
            for m in range(ML):
                v.reciprocal(G(m)[:, 86:87], G(m)[:, 85:86])
            for m in range(ML):
                v.reduce_max(G(m)[:, 16:20],
                             G(m)[:, 0:16].rearrange("p (a b) -> p a b", b=4),
                             axis=AX.X)
            for gi in range(4):
                for m in range(ML):
                    dst = G(m)[:, 20:24] if gi == 0 else G(m)[:, 24:28]
                    v.tensor_scalar(dst, G(m)[:, 16:20],
                                    G(m)[:, 16 + gi:17 + gi], None, OP.is_le)
                if gi > 0:
                    for m in range(ML):
                        v.tensor_add(G(m)[:, 20:24], G(m)[:, 20:24],
                                     G(m)[:, 24:28])
            for m in range(ML):
                v.tensor_scalar(G(m)[:, 28:32], G(m)[:, 20:24], 2.0, None,
                                OP.is_le)
            for m in range(ML):
                v.tensor_scalar(G(m)[:, 32:36], G(m)[:, 28:32], 1e4, -1e4,
                                OP.mult, OP.add)
            for e in range(E):
                gi = e // 4
                for m in range(ML):
                    v.scalar_tensor_tensor(G(m)[:, 36 + e:37 + e],
                                           G(m)[:, e:e + 1],
                                           G(m)[:, 28 + gi:29 + gi],
                                           G(m)[:, 32 + gi:33 + gi],
                                           OP.mult, OP.add)
            for f in range(E):
                for m in range(ML):
                    dst = G(m)[:, 52:68] if f == 0 else G(m)[:, 68:84]
                    v.tensor_scalar(dst, G(m)[:, 36:52],
                                    G(m)[:, 36 + f:37 + f], None, OP.is_le)
                if f > 0:
                    for m in range(ML):
                        v.tensor_add(G(m)[:, 52:68], G(m)[:, 52:68],
                                     G(m)[:, 68:84])
            for m in range(ML):
                v.tensor_scalar(G(m)[:, 68:84], G(m)[:, 52:68], 4.0, None,
                                OP.is_le)
            for m in range(ML):
                v.tensor_mul(c.comb[:, m * 16:(m + 1) * 16], G(m)[:, 0:16],
                             G(m)[:, 68:84])
            for m in range(ML):
                ins = v.tensor_scalar_mul(c.comb[:, m * 16:(m + 1) * 16],
                                          c.comb[:, m * 16:(m + 1) * 16],
                                          G(m)[:, 86:87])
            ins.then_inc(c.sem_vm, 1)
        vm[0] += ML * D_VM_M + 2

        if DEBUG:
            @block.sync
            def _(s):
                s.wait_ge(c.sem_vm, vm[0])
                s.wait_ge(c.sem_am, am[0])
                s.dma_start(out=c.dbg_comb, in_=c.comb[:]).then_inc(c.sem_sq, 16)
                s.dma_start(out=c.dbg_xtf, in_=c.xtf[:]).then_inc(c.sem_sq, 16)
                s.dma_start(out=c.dbg_x2, in_=c.x2_sc).then_inc(c.sem_sq, 16)
                s.dma_start(out=c.dbg_gsc, in_=c.gsc[:]).then_inc(c.sem_sq, 16)
            sq[0] += 64

        @block.sync
        def _(s):
            s.wait_ge(c.sem_sq, sq[0])

        @block.gpsimd
        def _(g):
            g.wait_ge(c.sem_gq, gq[0])

        barrier()

        if "E" not in PHASES:
            return c.nc
        E_TM0, E_VM0, E_AM0, E_GQ0 = tm[0], vm[0], am[0], gq[0]
        NE = E + NSH
        E_TM_E = ICN * 4 + ML * 4
        E_VM_E = ICN * 2 + ML * 4
        E_AM_E = ICN * 2

        def e_tm_hgrp(e, ic, grp):
            return E_TM0 + e * E_TM_E + ic * 4 + grp + 1

        def e_tm_w2(e, m, nc4):
            return E_TM0 + e * E_TM_E + ICN * 4 + m * 4 + nc4 + 1

        def e_vm_hev(e, ic, tch):
            return E_VM0 + 1 + e * E_VM_E + ic * 2 + tch + 1

        def e_vm_w2ev(e, m, nc4):
            return E_VM0 + 1 + e * E_VM_E + ICN * 2 + m * 4 + nc4 + 1

        def e_am_sil(e, ic, tch):
            return E_AM0 + e * E_AM_E + ic * 2 + tch + 1

        def ew(e):
            if e < E:
                return c.w1_t[e], c.w3_t[e], c.w2_t[e]
            return c.ws1_t[e - E], c.ws3_t[e - E], c.ws2_t[e - E]

        @block.gpsimd
        def _(g):
            for e in range(NE):
                w1s, w3s, w2s = ew(e)
                for ic in range(ICN):
                    par = ic % 2
                    if e > 0 or ic >= 2:
                        pe, pic = (e, ic - 2) if ic >= 2 else (e - 1, ICN - 2 + ic)
                        if ic < 2:
                            pic = ICN - 1 if ic == 0 else ICN - 2
                        g.wait_ge(c.sem_tm, e_tm_hgrp(pe, pic, 3))
                    g.dma_start(out=c.w1b[:, par * 2048:(par + 1) * 2048],
                                in_=w1s[ic].rearrange("p a i -> p (a i)")
                                ).then_inc(c.sem_gq, 16)
                    g.dma_start(out=c.w3b[:, par * 2048:(par + 1) * 2048],
                                in_=w3s[ic].rearrange("p a i -> p (a i)")
                                ).then_inc(c.sem_gq, 16)
                if e > 0:
                    g.wait_ge(c.sem_tm, e_tm_w2(e - 1, ML - 1, 3))
                for ic in range(ICN):
                    g.dma_start(out=c.w2a[:, ic * 2048:(ic + 1) * 2048],
                                in_=w2s[ic]).then_inc(c.sem_gq, 16)
        gq[0] += NE * ICN * 48

        @block.tensor
        def _(t):
            for e in range(NE):
                for ic in range(ICN):
                    par = ic % 2
                    t.wait_ge(c.sem_gq, E_GQ0 + (e * ICN + ic + 1) * 32
                              + e * ICN * 16)
                    for mat in range(2):
                        wb = c.w1b if mat == 0 else c.w3b
                        pb = pbA if mat == 0 else pbB
                        for tch in range(2):
                            if ic >= 1:
                                if mat == 0:
                                    t.wait_ge(c.sem_am, e_am_sil(e, ic - 1, tch))
                                else:
                                    t.wait_ge(c.sem_vm, e_vm_hev(e, ic - 1, tch))
                            elif e > 0:
                                if mat == 0:
                                    t.wait_ge(c.sem_am, e_am_sil(e - 1, ICN - 1, tch))
                                else:
                                    t.wait_ge(c.sem_vm, e_vm_hev(e - 1, ICN - 1, tch))
                        last = {}
                        for dcx in range(DC):
                            for tch in range(2):
                                last[tch] = t.matmul(
                                    pb[:, tch * 512:(tch + 1) * 512],
                                    wb[:, par * 2048 + dcx * 128:
                                       par * 2048 + (dcx + 1) * 128],
                                    c.xtf[:, dcx * 1024 + tch * 512:
                                          dcx * 1024 + (tch + 1) * 512],
                                    start=(dcx == 0), stop=(dcx == DC - 1))
                        last[0].then_inc(c.sem_tm, 1)
                        last[1].then_inc(c.sem_tm, 1)
                for m in range(ML):
                    for nc4 in range(4):
                        if m >= 1:
                            t.wait_ge(c.sem_vm, e_vm_w2ev(e, m - 1, nc4))
                        elif e > 0:
                            t.wait_ge(c.sem_vm, e_vm_w2ev(e - 1, ML - 1, nc4))
                    last = {}
                    for ic in range(ICN):
                        if m == 0:
                            t.wait_ge(c.sem_gq, E_GQ0 + e * ICN * 48
                                      + ICN * 32 + (ic + 1) * 16)
                        for nc4 in range(4):
                            pb = (pbC, pbC, pbD, pbD)[nc4]
                            half = nc4 % 2
                            last[nc4] = t.matmul(
                                pb[:, half * 512:(half + 1) * 512],
                                c.h[:, ic * 1024 + m * 128:
                                    ic * 1024 + (m + 1) * 128],
                                c.w2a[:, ic * 2048 + nc4 * 512:
                                      ic * 2048 + (nc4 + 1) * 512],
                                start=(ic == 0), stop=(ic == ICN - 1))
                    for nc4 in range(4):
                        last[nc4].then_inc(c.sem_tm, 1)
        tm[0] += NE * E_TM_E

        @block.scalar
        def _(a):
            for e in range(NE):
                for ic in range(ICN):
                    for tch in range(2):
                        a.wait_ge(c.sem_tm, e_tm_hgrp(e, ic, tch))
                        if ic >= 1:
                            a.wait_ge(c.sem_vm, e_vm_hev(e, ic - 1, tch))
                        elif e > 0:
                            a.wait_ge(c.sem_vm, e_vm_hev(e - 1, ICN - 1, tch))
                        a.activation(c.sil[:, tch * 512:(tch + 1) * 512],
                                     pbA[:, tch * 512:(tch + 1) * 512],
                                     AF.Silu).then_inc(c.sem_am, 1)
        am[0] += NE * E_AM_E

        @block.vector
        def _(v):
            v.memset(c.y[:], 0.0).then_inc(c.sem_vm, 1)
            for e in range(NE):
                for ic in range(ICN):
                    for tch in range(2):
                        v.wait_ge(c.sem_am, e_am_sil(e, ic, tch))
                        v.wait_ge(c.sem_tm, e_tm_hgrp(e, ic, 2 + tch))
                        v.tensor_mul(
                            c.h[:, ic * 1024 + tch * 512:
                                ic * 1024 + (tch + 1) * 512],
                            pbB[:, tch * 512:(tch + 1) * 512],
                            c.sil[:, tch * 512:(tch + 1) * 512]
                            ).then_inc(c.sem_vm, 1)
                for m in range(ML):
                    for nc4 in range(4):
                        pb = (pbC, pbC, pbD, pbD)[nc4]
                        half = nc4 % 2
                        v.wait_ge(c.sem_tm, e_tm_w2(e, m, nc4))
                        ysl = c.y[:, m * 2048 + nc4 * 512: m * 2048 + (nc4 + 1) * 512]
                        if e < E:
                            v.scalar_tensor_tensor(
                                ysl, pb[:, half * 512:(half + 1) * 512],
                                c.comb[:, m * 16 + e: m * 16 + e + 1],
                                ysl, OP.mult, OP.add).then_inc(c.sem_vm, 1)
                        else:
                            v.tensor_add(ysl, pb[:, half * 512:(half + 1) * 512],
                                         ysl).then_inc(c.sem_vm, 1)
        vm[0] += 1 + NE * E_VM_E

        if DEBUG:
            @block.sync
            def _(s):
                s.wait_ge(c.sem_vm, vm[0])
                s.dma_start(out=c.dbg_y, in_=c.y[:]).then_inc(c.sem_sq, 16)
            sq[0] += 16

        @block.gpsimd
        def _(g):
            g.wait_ge(c.sem_gq, gq[0])

        @block.sync
        def _(s):
            s.wait_ge(c.sem_sq, sq[0])

        barrier()

        if "F" not in PHASES:
            return c.nc
        F_TM0, F_VM0, F_AM0, F_SQ0 = tm[0], vm[0], am[0], sq[0]
        F_VM_M = 3
        F_AM_M = 1

        @block.sync
        def _(s):
            for m in range(ML):
                if m >= 1:
                    s.wait_ge(c.sem_vm, F_VM0 + m * F_VM_M)
                s.dma_start(out=c.x2f[:],
                            in_=c.x2_sc[m * 128:(m + 1) * 128, :]
                            ).then_inc(c.sem_sq, 16)
                s.wait_ge(c.sem_vm, F_VM0 + m * F_VM_M + 3)
                s.dma_start(out=c.out[m * 128:(m + 1) * 128, :],
                            in_=c.fout[:]).then_inc(c.sem_sq, 16)
            s.wait_ge(c.sem_sq, sq[0] + ML * 32)
        sq[0] += ML * 32

        @block.scalar
        def _(a):
            for m in range(ML):
                a.wait_ge(c.sem_vm, F_VM0 + m * F_VM_M + 1)
                a.activation(c.stat[:, c.ST_LNV:c.ST_LNV + 1],
                             c.stat[:, c.ST_VAR2:c.ST_VAR2 + 1],
                             AF.Ln, bias=c.stat[:, c.ST_EPS:c.ST_EPS + 1])
                a.activation(c.stat[:, 45:46], c.stat[:, c.ST_EPS:c.ST_EPS + 1],
                             AF.Square)
                a.activation(c.stat[:, c.ST_RSTD:c.ST_RSTD + 1],
                             c.stat[:, c.ST_LNV:c.ST_LNV + 1],
                             AF.Exp, scale=-0.5).then_inc(c.sem_am, 1)
        am[0] += ML * F_AM_M

        @block.vector
        def _(v):
            st = c.stat
            for m in range(ML):
                ysl = c.y[:, m * 2048:(m + 1) * 2048]
                for q in range(4):
                    v.bn_stats(st[:, 16 + q * 6:22 + q * 6],
                               ysl[:, q * 512:(q + 1) * 512])
                v.tensor_copy(c.rscr[:, 0:512], c.rscr[:, 512:1024])
                v.bn_aggr(st[:, 0:2], st[:, 16:40])
                v.tensor_copy(c.rscr[:, 0:512], c.rscr[:, 512:1024])
                v.tensor_copy(st[:, 12:14], st[:, 0:2]).then_inc(c.sem_vm, 1)
                v.wait_ge(c.sem_am, F_AM0 + m * F_AM_M + 1)
                if m >= 1:
                    v.wait_ge(c.sem_sq, F_SQ0 + m * 32)
                v.tensor_scalar(c.fout[:], ysl,
                                st[:, c.ST_MEAN2:c.ST_MEAN2 + 1],
                                st[:, c.ST_RSTD:c.ST_RSTD + 1],
                                OP.subtract, OP.mult).then_inc(c.sem_vm, 1)
                v.wait_ge(c.sem_sq, F_SQ0 + (m * 2 + 1) * 16)
                v.tensor_add(c.fout[:], c.fout[:], c.x2f[:]).then_inc(c.sem_vm, 1)
        vm[0] += ML * F_VM_M

        barrier()
    return c.nc


def _a_evac_vm(tr_i):
    j, dcx = divmod(tr_i, 16)
    return j * 18 + 2 + dcx + 1


def _prep_weights(inp):
    f32 = np.float32
    wq = np.ascontiguousarray(inp["L0_attn_Wq"], dtype=f32)
    wk = np.ascontiguousarray(inp["L0_attn_Wk"], dtype=f32)
    wv = np.ascontiguousarray(inp["L0_attn_Wv"], dtype=f32)
    wo = np.ascontiguousarray(inp["L0_attn_Wo"], dtype=f32)
    wg = np.ascontiguousarray(inp["L0_ffn_Wg"], dtype=f32)
    w1 = np.ascontiguousarray(inp["L0_ffn_W1"], dtype=f32)
    w3 = np.ascontiguousarray(inp["L0_ffn_W3"], dtype=f32)
    w2 = np.ascontiguousarray(inp["L0_ffn_W2"], dtype=f32)
    ws1 = np.ascontiguousarray(inp["L0_ffn_Ws1"], dtype=f32)
    ws3 = np.ascontiguousarray(inp["L0_ffn_Ws3"], dtype=f32)
    ws2 = np.ascontiguousarray(inp["L0_ffn_Ws2"], dtype=f32)

    def qkv_tile(w):
        return np.ascontiguousarray(
            w.T.reshape(DC, 128, DC, 128).transpose(1, 0, 2, 3))

    d = {}
    d["wq_t"] = qkv_tile(wq)
    d["wk_t"] = qkv_tile(wk)
    d["wv_t"] = qkv_tile(wv)
    d["wo_t"] = np.ascontiguousarray(wo.T.reshape(DC, 128, D))
    d["wg_t"] = np.ascontiguousarray(wg.T.reshape(DC, 128, E).transpose(1, 0, 2))
    d["w1_t"] = np.ascontiguousarray(
        w1.transpose(0, 2, 1).reshape(E, DC, 128, ICN, 128).transpose(0, 3, 2, 1, 4))
    d["w3_t"] = np.ascontiguousarray(
        w3.transpose(0, 2, 1).reshape(E, DC, 128, ICN, 128).transpose(0, 3, 2, 1, 4))
    d["w2_t"] = np.ascontiguousarray(
        w2.transpose(0, 2, 1).reshape(E, ICN, 128, D))
    d["ws1_t"] = np.ascontiguousarray(
        ws1.T.reshape(DC, 128, NSH * ICN, 128).transpose(2, 1, 0, 3)
        .reshape(NSH, ICN, 128, DC, 128))
    d["ws3_t"] = np.ascontiguousarray(
        ws3.T.reshape(DC, 128, NSH * ICN, 128).transpose(2, 1, 0, 3)
        .reshape(NSH, ICN, 128, DC, 128))
    d["ws2_t"] = np.ascontiguousarray(
        ws2.T.reshape(NSH, ICN, 128, D))
    return d


def kernel(**inputs):
    x = np.ascontiguousarray(np.asarray(inputs["L0_attn_x"], dtype=np.float32))
    x2d = x.reshape(T, D)
    wts = _prep_weights(inputs)

    ident = np.eye(128, dtype=np.float32).astype(ml_dtypes.bfloat16)

    c = build_nc()
    nc = build_program(c)

    xtiles = x2d.reshape(NT, 128, D)
    in_maps = []
    for core in range(NCORE):
        mask = np.zeros((128, 8 * 128), dtype=np.float32)
        for r in range(8):
            if r < core:
                pass
            elif r == core:
                blk = np.where(np.triu(np.ones((128, 128))) > 0, 0.0, -1e30)
                mask[:, r * 128:(r + 1) * 128] = blk.T
            else:
                mask[:, r * 128:(r + 1) * 128] = -1e30
        m = {
            "x_shard": np.ascontiguousarray(xtiles[core::8].reshape(TL, D)),
            "x_full": x2d,
            "mask8": mask.astype(ml_dtypes.bfloat16),
            "ident_in": ident,
        }
        m.update(wts)
        in_maps.append(m)

    res = run_bass_kernel_spmd(nc, in_maps, core_ids=list(range(NCORE)))
    global LAST_RES
    LAST_RES = res

    out = np.empty((NT, 128, D), dtype=np.float32)
    for core in range(NCORE):
        out[core::8] = res.results[core]["out"].reshape(ML, 128, D)
    return out.reshape(1, T, D)


# version 09
# speedup vs baseline: 1.0894x; 1.0399x over previous
import contextlib
import math

import numpy as np
import ml_dtypes

import concourse.bass as bass
import concourse.mybir as mybir
from concourse.bass_utils import run_bass_kernel_spmd
from concourse.alu_op_type import AluOpType as OP

F32 = mybir.dt.float32
F32R = mybir.dt.float32r
BF16 = mybir.dt.bfloat16
AF = mybir.ActivationFunctionType
AX = mybir.AxisListType

DEBUG = False
PHASES = "ABCDEF"
LAST_RES = None

NCORE = 8
T = 8192
D = 2048
NH = 16
HD = 128
DC = 16
NT = 64
ML = 8
TL = ML * 128
E = 16
ICN = 11
NSH = 2
EPS = 1e-5
CSH = 16.0
ISQ = 1.0 / math.sqrt(HD)

OFF_IDENT = 0
OFF_ONES = 256
OFF_ONESROW = 288
OFF_ONESROW_F = 544
OFF_MASK = 1056
OFF_STAT = 3104
OFF_COMB = 3360
OFF_GSC = 3872
OFF_WG = 4256
OFF_SCR = 4768
DYN = 8896
OFF_LROW = DYN + 184320
OFF_LROW_FR = DYN + 188416

OFF_WK = DYN
OFF_WV = DYN + 65536
OFF_XRAW = DYN + 131072
OFF_XLNT = DYN + 163840
OFF_XLNF = DYN + 167936
OFF_KVST = DYN + 184320
OFF_WQ = DYN
OFF_XLNOF = DYN + 65536
OFF_QFM = DYN + 98304
OFF_XRAWB = DYN + 131072
OFF_XLNTB = DYN + 163840
OFF_KBUF = DYN
OFF_VTM = DYN + 32768
OFF_PTB = DYN + 147456
OFF_ZFM = DYN + 151552
OFF_WO = DYN
OFF_AOUT = DYN + 65536
OFF_X2T = DYN + 73728
OFF_XRAWD = DYN + 81920
OFF_XTT = DYN + 90112
OFF_XTF = DYN + 98304
OFF_H = DYN
OFF_W1B = DYN + 22528
OFF_W3B = DYN + 30720
OFF_W2A = DYN + 38912
OFF_Y = DYN + 131072
OFF_X2F = DYN + 8192 - 8192
OFF_FOUT = DYN + 8192

SBUF_BASE = 16512
SBUF_END = SBUF_BASE + DYN + 196608
assert SBUF_END <= 229247, SBUF_END


def _sb(nc, name, shape, dtype, off):
    return nc.alloc_sbuf_tensor_at(name, shape, dtype, offset=SBUF_BASE + off)


class Ctx:
    pass


def build_nc():
    nc = bass.Bass("TRN2", target_bir_lowering=False)
    c = Ctx()
    c.nc = nc

    din = lambda n, s, dt=F32: nc.dram_tensor(n, s, dt, kind="ExternalInput").ap()
    c.x_shard = din("x_shard", [TL, D])
    c.x_full = din("x_full", [T, D])
    c.mask8 = din("mask8", [128, 8 * 128], BF16)
    c.ident_in = din("ident_in", [128, 128], BF16)
    c.wq_t = din("wq_t", [128, DC, DC, 128])
    c.wk_t = din("wk_t", [128, DC, DC, 128])
    c.wv_t = din("wv_t", [128, DC, DC, 128])
    c.wo_t = din("wo_t", [DC, 128, D])
    c.wg_t = din("wg_t", [128, DC, E])
    c.w1_t = din("w1_t", [E, ICN, 128, DC, 128])
    c.w3_t = din("w3_t", [E, ICN, 128, DC, 128])
    c.w2_t = din("w2_t", [E, ICN, 128, D])
    c.ws1_t = din("ws1_t", [NSH, ICN, 128, DC, 128])
    c.ws3_t = din("ws3_t", [NSH, ICN, 128, DC, 128])
    c.ws2_t = din("ws2_t", [NSH, ICN, 128, D])
    c.out = nc.dram_tensor("out", [TL, D], F32, kind="ExternalOutput").ap()
    c.k_sc = nc.dram_tensor("k_sc", [D, T], BF16, kind="Internal").ap()
    c.v_sc = nc.dram_tensor("v_sc", [4, T, 4 * HD], BF16, kind="Internal").ap()
    c.x2_sc = nc.dram_tensor("x2_sc", [TL, D], F32, kind="Internal").ap()
    if DEBUG:
        dout = lambda n, s, dt: nc.dram_tensor(n, s, dt, kind="ExternalOutput").ap()
        c.dbg_k = dout("dbg_k", [D, T], BF16)
        c.dbg_v = dout("dbg_v", [4, T, 4 * HD], BF16)
        c.dbg_q = dout("dbg_q", [128, NH * 1024], BF16)
        c.dbg_z = dout("dbg_z", [128, NH * 1024], BF16)
        c.dbg_comb = dout("dbg_comb", [128, 128], F32)
        c.dbg_xtf = dout("dbg_xtf", [128, DC * 1024], BF16)
        c.dbg_x2 = dout("dbg_x2", [TL, D], F32)
        c.dbg_y = dout("dbg_y", [128, ML * 2048], F32)
        c.dbg_gsc = dout("dbg_gsc", [128, 96], F32)
        c.dbg_wk = dout("dbg_wk", [128, 2048], BF16)
        c.dbg_xlnf = dout("dbg_xlnf", [128, DC * 512], BF16)
        c.dbg_stat = dout("dbg_stat", [128, 64], F32)
        c.dbg_xraw = dout("dbg_xraw", [128, 4 * 2048], F32)
        c.dbg_vtm = dout("dbg_vtm", [128, 8192], BF16)
        c.dbg_lrow = dout("dbg_lrow", [1, 1024], F32)
        c.dbg_rscr = dout("dbg_rscr", [128, 1024], F32)

    c.ident = _sb(nc, "ident", [128, 128], BF16, OFF_IDENT)
    c.ones = _sb(nc, "ones", [128, 1], BF16, OFF_ONES)
    c.onesrow = _sb(nc, "onesrow", [1, 128], BF16, OFF_ONESROW)
    c.onesrow_f = _sb(nc, "onesrow_f", [1, 128], F32, OFF_ONESROW_F)
    c.mask = _sb(nc, "mask", [128, 8 * 128], BF16, OFF_MASK)
    c.stat = _sb(nc, "stat", [128, 64], F32, OFF_STAT)
    c.comb = _sb(nc, "comb", [128, 128], F32, OFF_COMB)
    c.gsc = _sb(nc, "gsc", [128, 96], F32, OFF_GSC)
    c.gsc8 = _sb(nc, "gsc8", [128, 8 * 96], F32, 65536 + 8896)
    c.wg = _sb(nc, "wg", [128, DC * E], BF16, OFF_WG)
    c.lrow = _sb(nc, "lrow", [1, 1024], F32, OFF_LROW)
    c.lrow2 = _sb(nc, "lrow2", [1, 1024], F32, OFF_LROW_FR)
    c.rscr = _sb(nc, "rscr", [128, 1024], F32, OFF_SCR)
    c.sqs = _sb(nc, "sqs", [128, 1024], F32, OFF_SCR)
    c.sil = _sb(nc, "sil", [128, 1024], F32, OFF_SCR)

    c.wk = _sb(nc, "wk", [128, DC * DC * 128], BF16, OFF_WK)
    c.wv = _sb(nc, "wv", [128, DC * DC * 128], BF16, OFF_WV)
    c.xraw = _sb(nc, "xraw", [128, 4 * 2048], F32, OFF_XRAW)
    c.xlnt = _sb(nc, "xlnt", [128, 2048], BF16, OFF_XLNT)
    c.xlnf = _sb(nc, "xlnf", [128, DC * 512], BF16, OFF_XLNF)
    c.kvst = _sb(nc, "kvst", [128, 4 * 512], BF16, OFF_KVST)

    c.wq = _sb(nc, "wq", [128, DC * DC * 128], BF16, OFF_WQ)
    c.xlnof = _sb(nc, "xlnof", [128, DC * 1024], BF16, OFF_XLNOF)
    c.qfm = _sb(nc, "qfm", [128, NH * 1024], BF16, OFF_QFM)
    c.xrawb = _sb(nc, "xrawb", [128, 2 * 2048], F32, OFF_XRAWB)
    c.xlntb = _sb(nc, "xlntb", [128, 2048], BF16, OFF_XLNTB)

    c.kbuf = _sb(nc, "kbuf", [128, 2 * 8192], BF16, OFF_KBUF)
    c.vtm = _sb(nc, "vtm", [128, 2 * 8192], BF16, OFF_VTM)
    c.ptb = _sb(nc, "ptb", [128, 2 * 1024], BF16, OFF_PTB)
    c.zfm = _sb(nc, "zfm", [128, NH * 1024], BF16, OFF_ZFM)

    c.wo = _sb(nc, "wo", [128, DC * 2048], BF16, OFF_WO)
    c.aout = _sb(nc, "aout", [128, 2048], F32, OFF_AOUT)
    c.x2t = _sb(nc, "x2t", [128, 2048], F32, OFF_X2T)
    c.xrawd = _sb(nc, "xrawd", [128, 2048], F32, OFF_XRAWD)
    c.xtt = _sb(nc, "xtt", [128, 2048], BF16, OFF_XTT)
    c.xtf = _sb(nc, "xtf", [128, DC * 1024], BF16, OFF_XTF)

    c.h = _sb(nc, "h", [128, ICN * 1024], BF16, OFF_H)
    c.w1b = _sb(nc, "w1b", [128, 2 * 2048], BF16, OFF_W1B)
    c.w3b = _sb(nc, "w3b", [128, 2 * 2048], BF16, OFF_W3B)
    c.w2a = _sb(nc, "w2a", [128, ICN * 2048], BF16, OFF_W2A)
    c.y = _sb(nc, "y", [128, ML * 2048], F32, OFF_Y)

    c.x2f = _sb(nc, "x2f", [128, 2048], F32, DYN)
    c.fout = _sb(nc, "fout", [128, 2048], F32, DYN + 8192)

    c.ST_MEAN = 0
    c.ST_VAR = 1
    c.ST_RSTD = 2
    c.ST_LNV = 3
    c.ST_EPS = 4
    c.ST_NEGC = 5
    c.ST_NMAX = 6
    c.ST_SEXP = 7
    c.ST_RSE = 8
    c.ST_MEAN2 = 12
    c.ST_VAR2 = 13

    return c


def build_program(c):
    nc = c.nc
    stack = contextlib.ExitStack()
    with stack:
        sem = lambda n: stack.enter_context(nc.semaphore(n))
        c.sem_sq = sem("sq")
        c.sem_gq = sem("gq")
        c.sem_xq = sem("xq")
        c.sem_tm = sem("tm")
        c.sem_vm = sem("vm")
        c.sem_am = sem("am")
        c.sem_ph = sem("ph")
        pbA = stack.enter_context(nc.psum_tensor("pbA", [128, 1024], F32))
        pbB = stack.enter_context(nc.psum_tensor("pbB", [128, 1024], F32))
        pbC = stack.enter_context(nc.psum_tensor("pbC", [128, 1024], F32))
        pbD = stack.enter_context(nc.psum_tensor("pbD", [128, 1024], F32))
        block = stack.enter_context(nc.Block())

        sq = [0]
        gq = [0]
        xq = [0]
        tm = [0]
        vm = [0]
        am = [0]
        ph = [0]

        def barrier():
            ph[0] += 5
            tgt = ph[0]

            @block.sync
            def _(s):
                s.drain()
                s.sem_inc(c.sem_ph, 1)
                s.wait_ge(c.sem_ph, tgt)

            @block.tensor
            def _(t):
                t.drain()
                t.sem_inc(c.sem_ph, 1)
                t.wait_ge(c.sem_ph, tgt)

            @block.vector
            def _(v):
                v.drain()
                v.sem_inc(c.sem_ph, 1)
                v.wait_ge(c.sem_ph, tgt)

            @block.scalar
            def _(a):
                a.drain()
                a.sem_inc(c.sem_ph, 1)
                a.wait_ge(c.sem_ph, tgt)

            @block.gpsimd
            def _(g):
                g.drain()
                g.sem_inc(c.sem_ph, 1)
                g.wait_ge(c.sem_ph, tgt)

        @block.gpsimd
        def _(g):
            g.dma_start(out=c.wk[:], in_=c.wk_t.rearrange("p a b o -> p (a b o)")
                        ).then_inc(c.sem_gq, 16)
            g.dma_start(out=c.wv[:], in_=c.wv_t.rearrange("p a b o -> p (a b o)")
                        ).then_inc(c.sem_gq, 16)
            g.dma_start(out=c.wg[:], in_=c.wg_t.rearrange("p a e -> p (a e)")
                        ).then_inc(c.sem_gq, 16)
        gq[0] += 48
        GQ_PRE = gq[0]

        @block.sync
        def _(s):
            s.dma_start(out=c.mask[:], in_=c.mask8).then_inc(c.sem_sq, 16)
            s.dma_start(out=c.ident[:], in_=c.ident_in).then_inc(c.sem_sq, 16)
        sq[0] += 32
        SQ_PRE = sq[0]

        @block.vector
        def _(v):
            v.memset(c.ones[:], 1.0)
            v.memset(c.onesrow[:], 1.0)
            v.memset(c.stat[:, c.ST_EPS:c.ST_EPS + 1], EPS)
            v.memset(c.stat[:, c.ST_NEGC:c.ST_NEGC + 1], -CSH)
            v.memset(c.stat[:, 44:48], 0.0).then_inc(c.sem_vm, 1)
        vm[0] += 1

        x_t = c.x_full.rearrange("(s j p) d -> s p j d", p=128, j=4)

        A_TM_PER_SWEEP = 64 + 32
        A_VM_PER_SWEEP = 4 * (2 + 16) + 32
        A_AM_PER_SWEEP = 4
        A_TM0, A_VM0, A_AM0, A_SQ0, A_GQ0 = tm[0], vm[0], am[0], sq[0], gq[0]
        A_XQ0 = xq[0]

        @block.sync
        def _(s):
            for sw in range(16):
                for oc in range(DC):
                    s.wait_ge(c.sem_vm,
                              A_VM0 + sw * A_VM_PER_SWEEP + 72 + oc + 1)
                    slot = oc % 2
                    s.dma_start(
                        out=c.k_sc[oc * 128:(oc + 1) * 128, sw * 512:(sw + 1) * 512],
                        in_=c.kvst[:, slot * 512:(slot + 1) * 512],
                    ).then_inc(c.sem_sq, 16)
                for g in range(16):
                    t4, oc4 = divmod(g, 4)
                    s.wait_ge(c.sem_vm,
                              A_VM0 + sw * A_VM_PER_SWEEP + 88 + g + 1)
                    slot = 2 + g % 2
                    s.dma_start(
                        out=c.v_sc[oc4,
                                   sw * 512 + t4 * 128: sw * 512 + (t4 + 1) * 128,
                                   :],
                        in_=c.kvst[:, slot * 512:(slot + 1) * 512],
                    ).then_inc(c.sem_sq, 16)
        sq[0] += 16 * 32 * 16

        @block.scalar
        def _(a):
            for sw in range(16):
                for j in range(4):
                    if sw >= 1:
                        a.wait_ge(c.sem_vm, A_VM0 + (sw - 1) * A_VM_PER_SWEEP
                                  + j * 18 + 2)
                    a.dma_start(out=c.xraw[:, j * 2048:(j + 1) * 2048],
                                in_=x_t[sw][:, j]).then_inc(c.sem_xq, 16)
                for j in range(4):
                    a.wait_ge(c.sem_vm, A_VM0 + sw * A_VM_PER_SWEEP + j * 18 + 1)
                    a.activation(c.stat[:, c.ST_LNV:c.ST_LNV + 1],
                                 c.stat[:, c.ST_VAR2:c.ST_VAR2 + 1],
                                 AF.Ln, bias=c.stat[:, c.ST_EPS:c.ST_EPS + 1])
                    a.activation(c.stat[:, 45:46], c.stat[:, c.ST_EPS:c.ST_EPS + 1],
                                 AF.Square)
                    a.activation(c.stat[:, c.ST_RSTD:c.ST_RSTD + 1],
                                 c.stat[:, c.ST_LNV:c.ST_LNV + 1],
                                 AF.Exp, scale=-0.5).then_inc(c.sem_am, 1)
        am[0] += 16 * A_AM_PER_SWEEP
        xq[0] += 16 * 64

        @block.vector
        def _(v):
            st = c.stat
            for sw in range(16):
                for j in range(4):
                    v.wait_ge(c.sem_xq, A_XQ0 + (sw * 4 + j + 1) * 16)
                    src = c.xraw[:, j * 2048:(j + 1) * 2048]
                    for q in range(4):
                        v.bn_stats(st[:, 16 + q * 6:22 + q * 6],
                                   src[:, q * 512:(q + 1) * 512])
                    v.tensor_copy(c.rscr[:, 0:512], c.rscr[:, 512:1024])
                    v.bn_aggr(st[:, 0:2], st[:, 16:40])
                    v.tensor_copy(c.rscr[:, 0:512], c.rscr[:, 512:1024])
                    v.tensor_copy(st[:, 12:14], st[:, 0:2]).then_inc(c.sem_vm, 1)
                    v.wait_ge(c.sem_am, A_AM0 + sw * A_AM_PER_SWEEP + j + 1)
                    v.tensor_scalar(c.xlnt[:], src,
                                    st[:, c.ST_MEAN2:c.ST_MEAN2 + 1],
                                    st[:, c.ST_RSTD:c.ST_RSTD + 1],
                                    OP.subtract, OP.mult).then_inc(c.sem_vm, 1)
                    for dcx in range(DC):
                        tr_i = j * 16 + dcx
                        v.wait_ge(c.sem_tm, A_TM0 + sw * A_TM_PER_SWEEP + tr_i + 1)
                        slotv = pbC[:].bitcast(BF16)
                        srcv = (slotv[:, 0:128] if tr_i % 2 == 0
                                else slotv[:, 1024:1152])
                        v.tensor_copy(
                            c.xlnf[:, dcx * 512 + j * 128: dcx * 512 + (j + 1) * 128],
                            srcv).then_inc(c.sem_vm, 1)
                for oc in range(DC):
                    v.wait_ge(c.sem_tm,
                              A_TM0 + sw * A_TM_PER_SWEEP + 64 + oc + 1)
                    half = oc % 2
                    if sw >= 1 or oc >= 2:
                        if oc >= 2:
                            prev_store = sw * 32 + oc - 1
                        else:
                            prev_store = (sw - 1) * 32 + 15 + oc
                        v.wait_ge(c.sem_sq, A_SQ0 + prev_store * 16)
                    v.tensor_copy(
                        c.kvst[:, half * 512:(half + 1) * 512],
                        pbA[:, half * 512:(half + 1) * 512]).then_inc(c.sem_vm, 1)
                for g in range(16):
                    v.wait_ge(c.sem_tm,
                              A_TM0 + sw * A_TM_PER_SWEEP + 80 + g + 1)
                    half = g % 2
                    if sw >= 1 or g >= 2:
                        if g >= 2:
                            prev_store = sw * 32 + 15 + g
                        else:
                            prev_store = (sw - 1) * 32 + 31 + g
                        v.wait_ge(c.sem_sq, A_SQ0 + prev_store * 16)
                    v.tensor_copy(
                        c.kvst[:, (2 + half) * 512:(3 + half) * 512],
                        pbB[:, half * 512:(half + 1) * 512]).then_inc(c.sem_vm, 1)
        vm[0] += 16 * A_VM_PER_SWEEP

        @block.tensor
        def _(t):
            t.wait_ge(c.sem_sq, SQ_PRE)
            for sw in range(16):
                for j in range(4):
                    t.wait_ge(c.sem_vm, A_VM0 + sw * A_VM_PER_SWEEP + j * 18 + 2)
                    for dcx in range(DC):
                        tr_i = j * 16 + dcx
                        if tr_i >= 2:
                            t.wait_ge(c.sem_vm, A_VM0 + sw * A_VM_PER_SWEEP
                                      + _a_evac_vm(j * 16 + dcx - 2))
                        elif sw >= 1:
                            t.wait_ge(c.sem_vm, A_VM0 + (sw - 1) * A_VM_PER_SWEEP
                                      + _a_evac_vm(62 + tr_i))
                        slotv = pbC[:].bitcast(BF16)
                        dstv = (slotv[:, 0:128] if tr_i % 2 == 0
                                else slotv[:, 1024:1152])
                        t.transpose(dstv, c.xlnt[:, dcx * 128:(dcx + 1) * 128],
                                    c.ident[:]).then_inc(c.sem_tm, 1)
                if sw == 0:
                    t.wait_ge(c.sem_gq, GQ_PRE)
                t.wait_ge(c.sem_vm, A_VM0 + sw * A_VM_PER_SWEEP + 72)
                wk4 = c.wk[:].rearrange("p (a b o) -> p a b o", a=DC, b=DC)
                wv4 = c.wv[:].rearrange("p (a b o) -> p a b o", a=DC, b=DC)
                for oc in range(DC):
                    half = oc % 2
                    if oc >= 2:
                        t.wait_ge(c.sem_vm, A_VM0 + sw * A_VM_PER_SWEEP
                                  + 72 + oc - 2 + 1)
                    elif sw >= 1:
                        t.wait_ge(c.sem_vm, A_VM0 + (sw - 1) * A_VM_PER_SWEEP
                                  + 72 + 14 + oc + 1)
                    for dcx in range(DC):
                        ins = t.matmul(pbA[:, half * 512:(half + 1) * 512],
                                       wk4[:, dcx, oc, :],
                                       c.xlnf[:, dcx * 512:(dcx + 1) * 512],
                                       start=(dcx == 0), stop=(dcx == DC - 1))
                    ins.then_inc(c.sem_tm, 1)
                for g in range(16):
                    t4, oc4 = divmod(g, 4)
                    half = g % 2
                    if g >= 2:
                        t.wait_ge(c.sem_vm, A_VM0 + sw * A_VM_PER_SWEEP
                                  + 88 + g - 2 + 1)
                    elif sw >= 1:
                        t.wait_ge(c.sem_vm, A_VM0 + (sw - 1) * A_VM_PER_SWEEP
                                  + 88 + 14 + g + 1)
                    for dcx in range(DC):
                        ins = t.matmul(
                            pbB[:, half * 512:(half + 1) * 512],
                            c.xlnf[:, dcx * 512 + t4 * 128:
                                   dcx * 512 + (t4 + 1) * 128],
                            wv4[:, dcx, oc4 * 4:(oc4 + 1) * 4, :],
                            start=(dcx == 0), stop=(dcx == DC - 1))
                    ins.then_inc(c.sem_tm, 1)
        tm[0] += 16 * A_TM_PER_SWEEP

        if DEBUG:
            @block.sync
            def _(s):
                s.wait_ge(c.sem_vm, vm[0])
                s.dma_start(out=c.dbg_k, in_=c.k_sc).then_inc(c.sem_sq, 16)
                s.dma_start(out=c.dbg_v, in_=c.v_sc).then_inc(c.sem_sq, 16)
                s.dma_start(out=c.dbg_wk, in_=c.wk[:, 0:2048]).then_inc(c.sem_sq, 16)
                s.dma_start(out=c.dbg_xlnf, in_=c.xlnf[:]).then_inc(c.sem_sq, 16)
                s.dma_start(out=c.dbg_stat, in_=c.stat[:]).then_inc(c.sem_sq, 16)
                s.dma_start(out=c.dbg_xraw, in_=c.xraw[:]).then_inc(c.sem_sq, 16)
            sq[0] += 96

        @block.sync
        def _(s):
            s.wait_ge(c.sem_sq, sq[0])

        @block.gpsimd
        def _(g):
            g.wait_ge(c.sem_gq, gq[0])

        @block.scalar
        def _(a):
            a.wait_ge(c.sem_xq, xq[0])

        barrier()

        if "B" not in PHASES:
            return c.nc
        B_TM0, B_VM0, B_AM0, B_SQ0, B_GQ0 = tm[0], vm[0], am[0], sq[0], gq[0]
        B_VM_TILE = 2 + 16
        B_TM_TILE = 16

        @block.gpsimd
        def _(g):
            g.dma_start(out=c.wq[:], in_=c.wq_t.rearrange("p a b o -> p (a b o)")
                        ).then_inc(c.sem_gq, 16)
        gq[0] += 16

        @block.sync
        def _(s):
            for m in range(ML):
                if m >= 2:
                    s.wait_ge(c.sem_vm, B_VM0 + (m - 2) * B_VM_TILE + 2)
                par = m % 2
                s.dma_start(out=c.xrawb[:, par * 2048:(par + 1) * 2048],
                            in_=c.x_shard[m * 128:(m + 1) * 128, :]
                            ).then_inc(c.sem_sq, 16)
        sq[0] += ML * 16

        @block.scalar
        def _(a):
            for m in range(ML):
                a.wait_ge(c.sem_vm, B_VM0 + m * B_VM_TILE + 1)
                a.activation(c.stat[:, c.ST_LNV:c.ST_LNV + 1],
                             c.stat[:, c.ST_VAR2:c.ST_VAR2 + 1],
                             AF.Ln, bias=c.stat[:, c.ST_EPS:c.ST_EPS + 1])
                a.activation(c.stat[:, 45:46], c.stat[:, c.ST_EPS:c.ST_EPS + 1],
                             AF.Square)
                a.activation(c.stat[:, c.ST_RSTD:c.ST_RSTD + 1],
                             c.stat[:, c.ST_LNV:c.ST_LNV + 1],
                             AF.Exp, scale=-0.5).then_inc(c.sem_am, 1)
        am[0] += ML

        @block.vector
        def _(v):
            st = c.stat
            for m in range(ML):
                v.wait_ge(c.sem_sq, B_SQ0 + (m + 1) * 16)
                src = c.xrawb[:, (m % 2) * 2048:(m % 2 + 1) * 2048]
                for q in range(4):
                    v.bn_stats(st[:, 16 + q * 6:22 + q * 6],
                               src[:, q * 512:(q + 1) * 512])
                v.tensor_copy(c.rscr[:, 0:512], c.rscr[:, 512:1024])
                v.bn_aggr(st[:, 0:2], st[:, 16:40])
                v.tensor_copy(c.rscr[:, 0:512], c.rscr[:, 512:1024])
                v.tensor_copy(st[:, 12:14], st[:, 0:2]).then_inc(c.sem_vm, 1)
                v.wait_ge(c.sem_am, B_AM0 + m + 1)
                v.tensor_scalar(c.xlntb[:], src,
                                st[:, c.ST_MEAN2:c.ST_MEAN2 + 1],
                                st[:, c.ST_RSTD:c.ST_RSTD + 1],
                                OP.subtract, OP.mult).then_inc(c.sem_vm, 1)
                for dcx in range(DC):
                    v.wait_ge(c.sem_tm, B_TM0 + m * B_TM_TILE + dcx + 1)
                    slotv = pbC[:].bitcast(BF16)
                    srcv = (slotv[:, 0:128] if dcx % 2 == 0
                            else slotv[:, 1024:1152])
                    v.tensor_copy(
                        c.xlnof[:, dcx * 1024 + m * 128: dcx * 1024 + (m + 1) * 128],
                        srcv).then_inc(c.sem_vm, 1)
            for oc in range(NH):
                pb = pbA if oc % 2 == 0 else pbB
                v.wait_ge(c.sem_tm, B_TM0 + ML * B_TM_TILE + oc + 1)
                v.tensor_copy(c.qfm[:, oc * 1024:oc * 1024 + 512],
                              pb[:, 0:512]).then_inc(c.sem_vm, 1)
                v.tensor_copy(c.qfm[:, oc * 1024 + 512:(oc + 1) * 1024],
                              pb[:, 512:1024]).then_inc(c.sem_vm, 1)
        vm[0] += ML * B_VM_TILE + NH * 2

        @block.tensor
        def _(t):
            for m in range(ML):
                t.wait_ge(c.sem_vm, B_VM0 + m * B_VM_TILE + 2)
                for dcx in range(DC):
                    tr_i = m * 16 + dcx
                    if tr_i >= 2:
                        pm, pd = divmod(tr_i - 2, 16)
                        t.wait_ge(c.sem_vm, B_VM0 + pm * B_VM_TILE + 2 + pd + 1)
                    slotv = pbC[:].bitcast(BF16)
                    dstv = (slotv[:, 0:128] if dcx % 2 == 0
                            else slotv[:, 1024:1152])
                    t.transpose(dstv, c.xlntb[:, dcx * 128:(dcx + 1) * 128],
                                c.ident[:]).then_inc(c.sem_tm, 1)
            wq4 = c.wq[:].rearrange("p (a b o) -> p a b o", a=DC, b=DC)
            t.wait_ge(c.sem_gq, gq[0])
            t.wait_ge(c.sem_vm, B_VM0 + ML * B_VM_TILE)
            for oc in range(NH):
                pb = pbA if oc % 2 == 0 else pbB
                if oc >= 2:
                    t.wait_ge(c.sem_vm, B_VM0 + ML * B_VM_TILE + (oc - 2) * 2 + 2)
                for dcx in range(DC):
                    for tch in range(2):
                        ins = t.matmul(pb[:, tch * 512:(tch + 1) * 512],
                                       wq4[:, dcx, oc, :],
                                       c.xlnof[:, dcx * 1024 + tch * 512:
                                               dcx * 1024 + (tch + 1) * 512],
                                       start=(dcx == 0), stop=(dcx == DC - 1))
                ins.then_inc(c.sem_tm, 1)
        tm[0] += ML * B_TM_TILE + NH

        if DEBUG:
            @block.sync
            def _(s):
                s.wait_ge(c.sem_vm, vm[0])
                s.dma_start(out=c.dbg_q, in_=c.qfm[:]).then_inc(c.sem_sq, 16)
            sq[0] += 16

        SQ_B_END = sq[0]

        @block.sync
        def _(s):
            s.wait_ge(c.sem_tm, tm[0])
            s.dma_start(out=c.kbuf[:, 0:8192],
                        in_=c.k_sc[0:128, :]).then_inc(c.sem_sq, 16)
            for b8 in range(8):
                s.dma_start(
                    out=c.vtm[:, b8 * 1024:(b8 + 1) * 1024]
                    .rearrange("p (jj d) -> p jj d", d=128),
                    in_=c.v_sc[0, b8 * 1024:(b8 + 1) * 1024, 0:128]
                    .rearrange("(jj p) d -> p jj d", p=128)
                ).then_inc(c.sem_sq, 16)
        sq[0] += 144

        @block.sync
        def _(s):
            s.wait_ge(c.sem_sq, SQ_B_END)

        @block.gpsimd
        def _(g):
            g.wait_ge(c.sem_gq, gq[0])

        @block.scalar
        def _(a):
            a.wait_ge(c.sem_xq, xq[0])

        barrier()

        if "C" not in PHASES:
            return c.nc
        C_TM0, C_VM0, C_AM0, C_SQ0 = tm[0], vm[0], am[0], sq[0]
        C_TM_H = 64 + 64 + 1
        C_VM_H = 2
        C_AM_H = 64

        def c_tm_st(h, jj):
            return C_TM0 + h * C_TM_H + (1 if jj == 0 else 2 * jj)

        def c_tm_lpv(h, jj):
            if jj == 63:
                return C_TM0 + h * C_TM_H + 128
            return C_TM0 + h * C_TM_H + 2 * jj + 3

        def c_tm_bcast(h):
            return C_TM0 + (h + 1) * C_TM_H

        def c_vm_lrec(h):
            return C_VM0 + h * C_VM_H + 1

        def c_vm_zev(h):
            return C_VM0 + (h + 1) * C_VM_H

        def c_am_exp(h, jj):
            return C_AM0 + h * C_AM_H + jj + 1

        def parts_of(jj):
            a0 = (jj // 8) * 128
            p = []
            if a0 < 512:
                p.append((a0, 512))
            p.append((max(a0, 512), 1024))
            return p

        @block.sync
        def _(s):
            for h in range(1, NH):
                if h >= 2:
                    s.wait_ge(c.sem_tm, c_tm_lpv(h - 2, 63))
                s.dma_start(out=c.kbuf[:, (h % 2) * 8192:(h % 2 + 1) * 8192],
                            in_=c.k_sc[h * 128:(h + 1) * 128, :]
                            ).then_inc(c.sem_sq, 16)
                for b8 in range(8):
                    s.dma_start(
                        out=c.vtm[:, (h % 2) * 8192 + b8 * 1024:
                                  (h % 2) * 8192 + (b8 + 1) * 1024]
                        .rearrange("p (jj d) -> p jj d", d=128),
                        in_=c.v_sc[h // 4, b8 * 1024:(b8 + 1) * 1024,
                                   (h % 4) * 128:(h % 4 + 1) * 128]
                        .rearrange("(jj p) d -> p jj d", p=128)
                    ).then_inc(c.sem_sq, 16)
        sq[0] += (NH - 1) * 16 * 9

        @block.tensor
        def _(t):
            for h in range(NH):
                kb = c.kbuf[:, (h % 2) * 8192:(h % 2 + 1) * 8192]
                vt = c.vtm[:, (h % 2) * 8192:(h % 2 + 1) * 8192]
                t.wait_ge(c.sem_sq, C_SQ0 + 9 * h * 16)
                if h >= 1:
                    t.wait_ge(c.sem_am, C_AM0 + h * C_AM_H)
                    t.wait_ge(c.sem_vm, c_vm_zev(h - 1))
                qh = c.qfm[:, h * 1024:(h + 1) * 1024]

                def emit_st(jj):
                    stp = pbC if jj % 2 == 0 else pbD
                    if jj >= 2:
                        t.wait_ge(c.sem_am, c_am_exp(h, jj - 2))
                    a0 = (jj // 8) * 128
                    for (p0, p1) in parts_of(jj):
                        first_chunk = p0 <= a0 < p1
                        t.matmul(stp[:, p0:p1],
                                 kb[:, jj * 128:(jj + 1) * 128],
                                 qh[:, p0:p1], start=True,
                                 stop=not first_chunk)
                    r = jj % 8
                    ins = t.matmul(stp[:, a0:a0 + 128],
                                   c.mask[:, r * 128:(r + 1) * 128],
                                   c.ident[:], start=False, stop=True)
                    ins.then_inc(c.sem_tm, 1)

                def emit_lpv(jj):
                    ptp = c.ptb[:, (jj % 2) * 1024:(jj % 2 + 1) * 1024]
                    t.wait_ge(c.sem_am, c_am_exp(h, jj))
                    first = (jj == 0)
                    last = (jj == 63)
                    for (p0, p1) in parts_of(jj):
                        t.matmul(pbB[0:1, p0:p1], c.ones[:], ptp[:, p0:p1],
                                 start=first, stop=last)
                    for (p0, p1) in parts_of(jj):
                        ins = t.matmul(pbA[:, p0:p1],
                                       vt[:, jj * 128:(jj + 1) * 128],
                                       ptp[:, p0:p1], start=first, stop=last)
                    ins.then_inc(c.sem_tm, 1)

                emit_st(0)
                for jj in range(1, NT):
                    emit_st(jj)
                    emit_lpv(jj - 1)
                emit_lpv(63)
                t.wait_ge(c.sem_vm, c_vm_lrec(h))
                t.wait_ge(c.sem_am, c_am_exp(h, 63))
                for tch in range(2):
                    ins = t.matmul(pbC[:, tch * 512:(tch + 1) * 512],
                                   c.onesrow_f[:],
                                   c.lrow2[0:1, tch * 512:(tch + 1) * 512],
                                   start=True, stop=True)
                ins.then_inc(c.sem_tm, 1)
        tm[0] += NH * C_TM_H

        @block.scalar
        def _(a):
            for h in range(NH):
                for jj in range(NT):
                    stp = pbC if jj % 2 == 0 else pbD
                    ptp = c.ptb[:, (jj % 2) * 1024:(jj % 2 + 1) * 1024]
                    a0 = (jj // 8) * 128
                    a.wait_ge(c.sem_tm, c_tm_st(h, jj))
                    if jj >= 2:
                        a.wait_ge(c.sem_tm, c_tm_lpv(h, jj - 2))
                    elif h >= 1:
                        a.wait_ge(c.sem_tm, c_tm_lpv(h - 1, 62 + jj))
                    a.activation(ptp[:, a0:1024], stp[:, a0:1024], AF.Exp,
                                 bias=c.stat[:, c.ST_NEGC:c.ST_NEGC + 1],
                                 scale=ISQ).then_inc(c.sem_am, 1)
        am[0] += NH * C_AM_H

        @block.vector
        def _(v):
            v.memset(c.onesrow_f[:], 1.0)
            for h in range(NH):
                v.wait_ge(c.sem_tm, c_tm_lpv(h, 63))
                v.tensor_copy(c.lrow[0:1, :], pbB[0:1, :])
                v.reciprocal(c.lrow[0:1, :], c.lrow[0:1, :])
                v.tensor_copy(c.rscr[:, 0:512], c.rscr[:, 512:1024])
                v.tensor_copy(c.lrow2[0:1, :], c.lrow[0:1, :]).then_inc(c.sem_vm, 1)
                v.wait_ge(c.sem_tm, c_tm_bcast(h))
                v.tensor_copy(c.rscr[:], pbC[:])
                v.tensor_mul(
                    c.zfm[:, h * 1024: h * 1024 + 512],
                    pbA[:, 0:512], c.rscr[:, 0:512])
                v.tensor_mul(
                    c.zfm[:, h * 1024 + 512: (h + 1) * 1024],
                    pbA[:, 512:1024], c.rscr[:, 512:1024]).then_inc(c.sem_vm, 1)
        vm[0] += NH * C_VM_H

        if DEBUG:
            @block.sync
            def _(s):
                s.wait_ge(c.sem_vm, vm[0])
                s.dma_start(out=c.dbg_z, in_=c.zfm[:]).then_inc(c.sem_sq, 16)
                s.dma_start(out=c.dbg_vtm, in_=c.vtm[:, 8192:16384]).then_inc(c.sem_sq, 16)
                s.dma_start(out=c.dbg_lrow, in_=c.lrow[0:1, :]).then_inc(c.sem_sq, 16)
                s.dma_start(out=c.dbg_rscr, in_=c.rscr[:]).then_inc(c.sem_sq, 16)
            sq[0] += 64

        @block.sync
        def _(s):
            s.wait_ge(c.sem_sq, sq[0])

        barrier()

        if "D" not in PHASES:
            return c.nc
        D_TM0, D_VM0, D_AM0, D_SQ0, D_GQ0 = tm[0], vm[0], am[0], sq[0], gq[0]

        @block.gpsimd
        def _(g):
            g.dma_start(out=c.wo[:].rearrange("p (a o) -> p a o", a=DC),
                        in_=c.wo_t.rearrange("a p o -> p a o")
                        ).then_inc(c.sem_gq, 16)
        gq[0] += 16

        D_TM_M = 4 + 16
        D_VM_M = 5 + 16
        D_AM_M = 2

        @block.sync
        def _(s):
            for m in range(ML):
                s.dma_start(out=c.xrawd[:],
                            in_=c.x_shard[m * 128:(m + 1) * 128, :]
                            ).then_inc(c.sem_sq, 16)
                s.wait_ge(c.sem_vm, D_VM0 + m * D_VM_M + 3)
                s.dma_start(out=c.x2_sc[m * 128:(m + 1) * 128, :],
                            in_=c.x2t[:]).then_inc(c.sem_sq, 16)
        sq[0] += ML * 32

        @block.tensor
        def _(t):
            t.wait_ge(c.sem_gq, gq[0])
            wo3 = c.wo[:].rearrange("p (a o) -> p a o", a=DC)
            for m in range(ML):
                for nc4 in range(4):
                    pb = (pbA, pbA, pbB, pbB)[nc4]
                    half = nc4 % 2
                    if m >= 1:
                        t.wait_ge(c.sem_vm, D_VM0 + (m - 1) * D_VM_M + 1)
                    for zc in range(DC):
                        ins = t.matmul(pb[:, half * 512:(half + 1) * 512],
                                       c.zfm[:, zc * 1024 + m * 128:
                                             zc * 1024 + (m + 1) * 128],
                                       wo3[:, zc, nc4 * 512:(nc4 + 1) * 512],
                                       start=(zc == 0), stop=(zc == DC - 1))
                    ins.then_inc(c.sem_tm, 1)
                t.wait_ge(c.sem_vm, D_VM0 + m * D_VM_M + 5)
                for dcx in range(DC):
                    tr_i = m * 16 + dcx
                    if tr_i >= 2:
                        pm, pd = divmod(tr_i - 2, 16)
                        t.wait_ge(c.sem_vm, D_VM0 + pm * D_VM_M + 5 + pd + 1)
                    slotv = pbD[:].bitcast(BF16)
                    dstv = (slotv[:, 0:128] if dcx % 2 == 0
                            else slotv[:, 1024:1152])
                    t.transpose(dstv, c.xtt[:, dcx * 128:(dcx + 1) * 128],
                                c.ident[:]).then_inc(c.sem_tm, 1)
            wg3 = c.wg[:].rearrange("p (a e) -> p a e", a=DC)
            t.wait_ge(c.sem_vm, D_VM0 + ML * D_VM_M)
            for m in range(ML):
                for dcx in range(DC):
                    ins = t.matmul(pbC[:, m * 16:(m + 1) * 16],
                                   c.xtf[:, dcx * 1024 + m * 128:
                                         dcx * 1024 + (m + 1) * 128],
                                   wg3[:, dcx, :],
                                   start=(dcx == 0), stop=(dcx == DC - 1))
            ins.then_inc(c.sem_tm, 1)
        tm[0] += ML * D_TM_M + 1

        @block.scalar
        def _(a):
            for m in range(ML):
                a.wait_ge(c.sem_vm, D_VM0 + m * D_VM_M + 2)
                a.activation(c.stat[:, c.ST_LNV:c.ST_LNV + 1],
                             c.stat[:, c.ST_VAR2:c.ST_VAR2 + 1],
                             AF.Ln, bias=c.stat[:, c.ST_EPS:c.ST_EPS + 1])
                a.activation(c.stat[:, 45:46], c.stat[:, c.ST_EPS:c.ST_EPS + 1],
                             AF.Square)
                a.activation(c.stat[:, c.ST_RSTD:c.ST_RSTD + 1],
                             c.stat[:, c.ST_LNV:c.ST_LNV + 1],
                             AF.Exp, scale=-0.5).then_inc(c.sem_am, 1)
                a.wait_ge(c.sem_vm, D_VM0 + m * D_VM_M + 4)
                a.activation(c.stat[:, c.ST_LNV:c.ST_LNV + 1],
                             c.stat[:, c.ST_VAR2:c.ST_VAR2 + 1],
                             AF.Ln, bias=c.stat[:, c.ST_EPS:c.ST_EPS + 1])
                a.activation(c.stat[:, 45:46], c.stat[:, c.ST_EPS:c.ST_EPS + 1],
                             AF.Square)
                a.activation(c.stat[:, c.ST_RSTD:c.ST_RSTD + 1],
                             c.stat[:, c.ST_LNV:c.ST_LNV + 1],
                             AF.Exp, scale=-0.5).then_inc(c.sem_am, 1)
            a.wait_ge(c.sem_vm, D_VM0 + ML * D_VM_M + 1)
            for m in range(ML):
                g8 = c.gsc8[:, m * 96:(m + 1) * 96]
                ins = a.activation(g8[:, 0:16], pbC[:, m * 16:(m + 1) * 16],
                                   AF.Exp, bias=g8[:, 84:85])
            ins.then_inc(c.sem_am, 1)
        am[0] += ML * D_AM_M + 1

        @block.vector
        def _(v):
            st = c.stat
            for m in range(ML):
                for nc4 in range(4):
                    pb = (pbA, pbA, pbB, pbB)[nc4]
                    half = nc4 % 2
                    v.wait_ge(c.sem_tm, D_TM0 + m * D_TM_M + nc4 + 1)
                    ins = v.tensor_copy(c.aout[:, nc4 * 512:(nc4 + 1) * 512],
                                        pb[:, half * 512:(half + 1) * 512])
                ins.then_inc(c.sem_vm, 1)
                for q in range(4):
                    v.bn_stats(st[:, 16 + q * 6:22 + q * 6],
                               c.aout[:, q * 512:(q + 1) * 512])
                v.tensor_copy(c.rscr[:, 0:512], c.rscr[:, 512:1024])
                v.bn_aggr(st[:, 0:2], st[:, 16:40])
                v.tensor_copy(c.rscr[:, 0:512], c.rscr[:, 512:1024])
                v.tensor_copy(st[:, 12:14], st[:, 0:2]).then_inc(c.sem_vm, 1)
                v.wait_ge(c.sem_am, D_AM0 + m * D_AM_M + 1)
                v.tensor_scalar(c.x2t[:], c.aout[:],
                                st[:, c.ST_MEAN2:c.ST_MEAN2 + 1],
                                st[:, c.ST_RSTD:c.ST_RSTD + 1],
                                OP.subtract, OP.mult)
                v.wait_ge(c.sem_sq, D_SQ0 + (m * 2 + 1) * 16)
                v.tensor_add(c.x2t[:], c.x2t[:], c.xrawd[:]).then_inc(c.sem_vm, 1)
                for q in range(4):
                    v.bn_stats(st[:, 16 + q * 6:22 + q * 6],
                               c.x2t[:, q * 512:(q + 1) * 512])
                v.tensor_copy(c.rscr[:, 0:512], c.rscr[:, 512:1024])
                v.bn_aggr(st[:, 0:2], st[:, 16:40])
                v.tensor_copy(c.rscr[:, 0:512], c.rscr[:, 512:1024])
                v.tensor_copy(st[:, 12:14], st[:, 0:2]).then_inc(c.sem_vm, 1)
                v.wait_ge(c.sem_am, D_AM0 + m * D_AM_M + 2)
                v.tensor_scalar(c.xtt[:], c.x2t[:],
                                st[:, c.ST_MEAN2:c.ST_MEAN2 + 1],
                                st[:, c.ST_RSTD:c.ST_RSTD + 1],
                                OP.subtract, OP.mult).then_inc(c.sem_vm, 1)
                for dcx in range(DC):
                    v.wait_ge(c.sem_tm, D_TM0 + m * D_TM_M + 4 + dcx + 1)
                    slotv = pbD[:].bitcast(BF16)
                    srcv = (slotv[:, 0:128] if dcx % 2 == 0
                            else slotv[:, 1024:1152])
                    v.tensor_copy(
                        c.xtf[:, dcx * 1024 + m * 128: dcx * 1024 + (m + 1) * 128],
                        srcv).then_inc(c.sem_vm, 1)
            G = lambda m: c.gsc8[:, m * 96:(m + 1) * 96]
            v.wait_ge(c.sem_tm, D_TM0 + ML * D_TM_M + 1)
            for m in range(ML):
                ins = v.reduce_max(G(m)[:, 84:85], pbC[:, m * 16:(m + 1) * 16],
                                   axis=AX.X, negate=True)
            ins.then_inc(c.sem_vm, 1)
            v.wait_ge(c.sem_am, D_AM0 + ML * D_AM_M + 1)
            for m in range(ML):
                v.reduce_sum(G(m)[:, 85:86], G(m)[:, 0:16], axis=AX.X)
            for m in range(ML):
                v.reciprocal(G(m)[:, 86:87], G(m)[:, 85:86])
            for m in range(ML):
                v.reduce_max(G(m)[:, 16:20],
                             G(m)[:, 0:16].rearrange("p (a b) -> p a b", b=4),
                             axis=AX.X)
            for gi in range(4):
                for m in range(ML):
                    dst = G(m)[:, 20:24] if gi == 0 else G(m)[:, 24:28]
                    v.tensor_scalar(dst, G(m)[:, 16:20],
                                    G(m)[:, 16 + gi:17 + gi], None, OP.is_le)
                if gi > 0:
                    for m in range(ML):
                        v.tensor_add(G(m)[:, 20:24], G(m)[:, 20:24],
                                     G(m)[:, 24:28])
            for m in range(ML):
                v.tensor_scalar(G(m)[:, 28:32], G(m)[:, 20:24], 2.0, None,
                                OP.is_le)
            for m in range(ML):
                v.tensor_scalar(G(m)[:, 32:36], G(m)[:, 28:32], 1e4, -1e4,
                                OP.mult, OP.add)
            for e in range(E):
                gi = e // 4
                for m in range(ML):
                    v.scalar_tensor_tensor(G(m)[:, 36 + e:37 + e],
                                           G(m)[:, e:e + 1],
                                           G(m)[:, 28 + gi:29 + gi],
                                           G(m)[:, 32 + gi:33 + gi],
                                           OP.mult, OP.add)
            for f in range(E):
                for m in range(ML):
                    dst = G(m)[:, 52:68] if f == 0 else G(m)[:, 68:84]
                    v.tensor_scalar(dst, G(m)[:, 36:52],
                                    G(m)[:, 36 + f:37 + f], None, OP.is_le)
                if f > 0:
                    for m in range(ML):
                        v.tensor_add(G(m)[:, 52:68], G(m)[:, 52:68],
                                     G(m)[:, 68:84])
            for m in range(ML):
                v.tensor_scalar(G(m)[:, 68:84], G(m)[:, 52:68], 4.0, None,
                                OP.is_le)
            for m in range(ML):
                v.tensor_mul(c.comb[:, m * 16:(m + 1) * 16], G(m)[:, 0:16],
                             G(m)[:, 68:84])
            for m in range(ML):
                ins = v.tensor_scalar_mul(c.comb[:, m * 16:(m + 1) * 16],
                                          c.comb[:, m * 16:(m + 1) * 16],
                                          G(m)[:, 86:87])
            ins.then_inc(c.sem_vm, 1)
        vm[0] += ML * D_VM_M + 2

        if DEBUG:
            @block.sync
            def _(s):
                s.wait_ge(c.sem_vm, vm[0])
                s.wait_ge(c.sem_am, am[0])
                s.dma_start(out=c.dbg_comb, in_=c.comb[:]).then_inc(c.sem_sq, 16)
                s.dma_start(out=c.dbg_xtf, in_=c.xtf[:]).then_inc(c.sem_sq, 16)
                s.dma_start(out=c.dbg_x2, in_=c.x2_sc).then_inc(c.sem_sq, 16)
                s.dma_start(out=c.dbg_gsc, in_=c.gsc[:]).then_inc(c.sem_sq, 16)
            sq[0] += 64

        @block.sync
        def _(s):
            s.wait_ge(c.sem_sq, sq[0])

        @block.gpsimd
        def _(g):
            g.wait_ge(c.sem_gq, gq[0])

        barrier()

        if "E" not in PHASES:
            return c.nc
        E_TM0, E_VM0, E_AM0, E_GQ0 = tm[0], vm[0], am[0], gq[0]
        NE = E + NSH
        E_TM_E = ICN * 4 + ML * 4
        E_VM_E = ICN * 2 + ML * 4
        E_AM_E = ICN * 2

        def e_tm_hgrp(e, ic, grp):
            return E_TM0 + e * E_TM_E + ic * 4 + grp + 1

        def e_tm_w2(e, m, nc4):
            return E_TM0 + e * E_TM_E + ICN * 4 + m * 4 + nc4 + 1

        def e_vm_hev(e, ic, tch):
            return E_VM0 + 1 + e * E_VM_E + ic * 2 + tch + 1

        def e_vm_w2ev(e, m, nc4):
            return E_VM0 + 1 + e * E_VM_E + ICN * 2 + m * 4 + nc4 + 1

        def e_am_sil(e, ic, tch):
            return E_AM0 + e * E_AM_E + ic * 2 + tch + 1

        def ew(e):
            if e < E:
                return c.w1_t[e], c.w3_t[e], c.w2_t[e]
            return c.ws1_t[e - E], c.ws3_t[e - E], c.ws2_t[e - E]

        @block.gpsimd
        def _(g):
            for e in range(NE):
                w1s, w3s, w2s = ew(e)
                for ic in range(ICN):
                    par = ic % 2
                    if e > 0 or ic >= 2:
                        pe, pic = (e, ic - 2) if ic >= 2 else (e - 1, ICN - 2 + ic)
                        if ic < 2:
                            pic = ICN - 1 if ic == 0 else ICN - 2
                        g.wait_ge(c.sem_tm, e_tm_hgrp(pe, pic, 3))
                    g.dma_start(out=c.w1b[:, par * 2048:(par + 1) * 2048],
                                in_=w1s[ic].rearrange("p a i -> p (a i)")
                                ).then_inc(c.sem_gq, 16)
                    g.dma_start(out=c.w3b[:, par * 2048:(par + 1) * 2048],
                                in_=w3s[ic].rearrange("p a i -> p (a i)")
                                ).then_inc(c.sem_gq, 16)
                if e > 0:
                    g.wait_ge(c.sem_tm, e_tm_w2(e - 1, ML - 1, 3))
                for ic in range(ICN):
                    g.dma_start(out=c.w2a[:, ic * 2048:(ic + 1) * 2048],
                                in_=w2s[ic]).then_inc(c.sem_gq, 16)
        gq[0] += NE * ICN * 48

        @block.tensor
        def _(t):
            for e in range(NE):
                for ic in range(ICN):
                    par = ic % 2
                    t.wait_ge(c.sem_gq, E_GQ0 + (e * ICN + ic + 1) * 32
                              + e * ICN * 16)
                    for mat in range(2):
                        wb = c.w1b if mat == 0 else c.w3b
                        pb = pbA if mat == 0 else pbB
                        for tch in range(2):
                            if ic >= 1:
                                if mat == 0:
                                    t.wait_ge(c.sem_am, e_am_sil(e, ic - 1, tch))
                                else:
                                    t.wait_ge(c.sem_vm, e_vm_hev(e, ic - 1, tch))
                            elif e > 0:
                                if mat == 0:
                                    t.wait_ge(c.sem_am, e_am_sil(e - 1, ICN - 1, tch))
                                else:
                                    t.wait_ge(c.sem_vm, e_vm_hev(e - 1, ICN - 1, tch))
                        last = {}
                        for dcx in range(DC):
                            for tch in range(2):
                                last[tch] = t.matmul(
                                    pb[:, tch * 512:(tch + 1) * 512],
                                    wb[:, par * 2048 + dcx * 128:
                                       par * 2048 + (dcx + 1) * 128],
                                    c.xtf[:, dcx * 1024 + tch * 512:
                                          dcx * 1024 + (tch + 1) * 512],
                                    start=(dcx == 0), stop=(dcx == DC - 1))
                        last[0].then_inc(c.sem_tm, 1)
                        last[1].then_inc(c.sem_tm, 1)
                for m in range(ML):
                    for nc4 in range(4):
                        if m >= 1:
                            t.wait_ge(c.sem_vm, e_vm_w2ev(e, m - 1, nc4))
                        elif e > 0:
                            t.wait_ge(c.sem_vm, e_vm_w2ev(e - 1, ML - 1, nc4))
                    last = {}
                    for ic in range(ICN):
                        if m == 0:
                            t.wait_ge(c.sem_gq, E_GQ0 + e * ICN * 48
                                      + ICN * 32 + (ic + 1) * 16)
                        for nc4 in range(4):
                            pb = (pbC, pbC, pbD, pbD)[nc4]
                            half = nc4 % 2
                            last[nc4] = t.matmul(
                                pb[:, half * 512:(half + 1) * 512],
                                c.h[:, ic * 1024 + m * 128:
                                    ic * 1024 + (m + 1) * 128],
                                c.w2a[:, ic * 2048 + nc4 * 512:
                                      ic * 2048 + (nc4 + 1) * 512],
                                start=(ic == 0), stop=(ic == ICN - 1))
                    for nc4 in range(4):
                        last[nc4].then_inc(c.sem_tm, 1)
        tm[0] += NE * E_TM_E

        @block.scalar
        def _(a):
            for e in range(NE):
                for ic in range(ICN):
                    for tch in range(2):
                        a.wait_ge(c.sem_tm, e_tm_hgrp(e, ic, tch))
                        if ic >= 1:
                            a.wait_ge(c.sem_vm, e_vm_hev(e, ic - 1, tch))
                        elif e > 0:
                            a.wait_ge(c.sem_vm, e_vm_hev(e - 1, ICN - 1, tch))
                        a.activation(c.sil[:, tch * 512:(tch + 1) * 512],
                                     pbA[:, tch * 512:(tch + 1) * 512],
                                     AF.Silu).then_inc(c.sem_am, 1)
        am[0] += NE * E_AM_E

        @block.vector
        def _(v):
            v.memset(c.y[:], 0.0).then_inc(c.sem_vm, 1)
            for e in range(NE):
                for ic in range(ICN):
                    for tch in range(2):
                        v.wait_ge(c.sem_am, e_am_sil(e, ic, tch))
                        v.wait_ge(c.sem_tm, e_tm_hgrp(e, ic, 2 + tch))
                        v.tensor_mul(
                            c.h[:, ic * 1024 + tch * 512:
                                ic * 1024 + (tch + 1) * 512],
                            pbB[:, tch * 512:(tch + 1) * 512],
                            c.sil[:, tch * 512:(tch + 1) * 512]
                            ).then_inc(c.sem_vm, 1)
                for m in range(ML):
                    for nc4 in range(4):
                        pb = (pbC, pbC, pbD, pbD)[nc4]
                        half = nc4 % 2
                        v.wait_ge(c.sem_tm, e_tm_w2(e, m, nc4))
                        ysl = c.y[:, m * 2048 + nc4 * 512: m * 2048 + (nc4 + 1) * 512]
                        if e < E:
                            v.scalar_tensor_tensor(
                                ysl, pb[:, half * 512:(half + 1) * 512],
                                c.comb[:, m * 16 + e: m * 16 + e + 1],
                                ysl, OP.mult, OP.add).then_inc(c.sem_vm, 1)
                        else:
                            v.tensor_add(ysl, pb[:, half * 512:(half + 1) * 512],
                                         ysl).then_inc(c.sem_vm, 1)
        vm[0] += 1 + NE * E_VM_E

        if DEBUG:
            @block.sync
            def _(s):
                s.wait_ge(c.sem_vm, vm[0])
                s.dma_start(out=c.dbg_y, in_=c.y[:]).then_inc(c.sem_sq, 16)
            sq[0] += 16

        @block.gpsimd
        def _(g):
            g.wait_ge(c.sem_gq, gq[0])

        @block.sync
        def _(s):
            s.wait_ge(c.sem_sq, sq[0])

        barrier()

        if "F" not in PHASES:
            return c.nc
        F_TM0, F_VM0, F_AM0, F_SQ0 = tm[0], vm[0], am[0], sq[0]
        F_VM_M = 3
        F_AM_M = 1

        @block.sync
        def _(s):
            for m in range(ML):
                if m >= 1:
                    s.wait_ge(c.sem_vm, F_VM0 + m * F_VM_M)
                s.dma_start(out=c.x2f[:],
                            in_=c.x2_sc[m * 128:(m + 1) * 128, :]
                            ).then_inc(c.sem_sq, 16)
                s.wait_ge(c.sem_vm, F_VM0 + m * F_VM_M + 3)
                s.dma_start(out=c.out[m * 128:(m + 1) * 128, :],
                            in_=c.fout[:]).then_inc(c.sem_sq, 16)
            s.wait_ge(c.sem_sq, sq[0] + ML * 32)
        sq[0] += ML * 32

        @block.scalar
        def _(a):
            for m in range(ML):
                a.wait_ge(c.sem_vm, F_VM0 + m * F_VM_M + 1)
                a.activation(c.stat[:, c.ST_LNV:c.ST_LNV + 1],
                             c.stat[:, c.ST_VAR2:c.ST_VAR2 + 1],
                             AF.Ln, bias=c.stat[:, c.ST_EPS:c.ST_EPS + 1])
                a.activation(c.stat[:, 45:46], c.stat[:, c.ST_EPS:c.ST_EPS + 1],
                             AF.Square)
                a.activation(c.stat[:, c.ST_RSTD:c.ST_RSTD + 1],
                             c.stat[:, c.ST_LNV:c.ST_LNV + 1],
                             AF.Exp, scale=-0.5).then_inc(c.sem_am, 1)
        am[0] += ML * F_AM_M

        @block.vector
        def _(v):
            st = c.stat
            for m in range(ML):
                ysl = c.y[:, m * 2048:(m + 1) * 2048]
                for q in range(4):
                    v.bn_stats(st[:, 16 + q * 6:22 + q * 6],
                               ysl[:, q * 512:(q + 1) * 512])
                v.tensor_copy(c.rscr[:, 0:512], c.rscr[:, 512:1024])
                v.bn_aggr(st[:, 0:2], st[:, 16:40])
                v.tensor_copy(c.rscr[:, 0:512], c.rscr[:, 512:1024])
                v.tensor_copy(st[:, 12:14], st[:, 0:2]).then_inc(c.sem_vm, 1)
                v.wait_ge(c.sem_am, F_AM0 + m * F_AM_M + 1)
                if m >= 1:
                    v.wait_ge(c.sem_sq, F_SQ0 + m * 32)
                v.tensor_scalar(c.fout[:], ysl,
                                st[:, c.ST_MEAN2:c.ST_MEAN2 + 1],
                                st[:, c.ST_RSTD:c.ST_RSTD + 1],
                                OP.subtract, OP.mult).then_inc(c.sem_vm, 1)
                v.wait_ge(c.sem_sq, F_SQ0 + (m * 2 + 1) * 16)
                v.tensor_add(c.fout[:], c.fout[:], c.x2f[:]).then_inc(c.sem_vm, 1)
        vm[0] += ML * F_VM_M

        barrier()
    return c.nc


def _a_evac_vm(tr_i):
    j, dcx = divmod(tr_i, 16)
    return j * 18 + 2 + dcx + 1


def _prep_weights(inp):
    f32 = np.float32
    wq = np.ascontiguousarray(inp["L0_attn_Wq"], dtype=f32)
    wk = np.ascontiguousarray(inp["L0_attn_Wk"], dtype=f32)
    wv = np.ascontiguousarray(inp["L0_attn_Wv"], dtype=f32)
    wo = np.ascontiguousarray(inp["L0_attn_Wo"], dtype=f32)
    wg = np.ascontiguousarray(inp["L0_ffn_Wg"], dtype=f32)
    w1 = np.ascontiguousarray(inp["L0_ffn_W1"], dtype=f32)
    w3 = np.ascontiguousarray(inp["L0_ffn_W3"], dtype=f32)
    w2 = np.ascontiguousarray(inp["L0_ffn_W2"], dtype=f32)
    ws1 = np.ascontiguousarray(inp["L0_ffn_Ws1"], dtype=f32)
    ws3 = np.ascontiguousarray(inp["L0_ffn_Ws3"], dtype=f32)
    ws2 = np.ascontiguousarray(inp["L0_ffn_Ws2"], dtype=f32)

    def qkv_tile(w):
        return np.ascontiguousarray(
            w.T.reshape(DC, 128, DC, 128).transpose(1, 0, 2, 3))

    d = {}
    d["wq_t"] = qkv_tile(wq)
    d["wk_t"] = qkv_tile(wk)
    d["wv_t"] = qkv_tile(wv)
    d["wo_t"] = np.ascontiguousarray(wo.T.reshape(DC, 128, D))
    d["wg_t"] = np.ascontiguousarray(wg.T.reshape(DC, 128, E).transpose(1, 0, 2))
    d["w1_t"] = np.ascontiguousarray(
        w1.transpose(0, 2, 1).reshape(E, DC, 128, ICN, 128).transpose(0, 3, 2, 1, 4))
    d["w3_t"] = np.ascontiguousarray(
        w3.transpose(0, 2, 1).reshape(E, DC, 128, ICN, 128).transpose(0, 3, 2, 1, 4))
    d["w2_t"] = np.ascontiguousarray(
        w2.transpose(0, 2, 1).reshape(E, ICN, 128, D))
    d["ws1_t"] = np.ascontiguousarray(
        ws1.T.reshape(DC, 128, NSH * ICN, 128).transpose(2, 1, 0, 3)
        .reshape(NSH, ICN, 128, DC, 128))
    d["ws3_t"] = np.ascontiguousarray(
        ws3.T.reshape(DC, 128, NSH * ICN, 128).transpose(2, 1, 0, 3)
        .reshape(NSH, ICN, 128, DC, 128))
    d["ws2_t"] = np.ascontiguousarray(
        ws2.T.reshape(NSH, ICN, 128, D))
    return d


def kernel(**inputs):
    x = np.ascontiguousarray(np.asarray(inputs["L0_attn_x"], dtype=np.float32))
    x2d = x.reshape(T, D)
    wts = _prep_weights(inputs)

    ident = np.eye(128, dtype=np.float32).astype(ml_dtypes.bfloat16)

    c = build_nc()
    nc = build_program(c)

    xtiles = x2d.reshape(NT, 128, D)
    in_maps = []
    for core in range(NCORE):
        mask = np.zeros((128, 8 * 128), dtype=np.float32)
        for r in range(8):
            if r < core:
                pass
            elif r == core:
                blk = np.where(np.triu(np.ones((128, 128))) > 0, 0.0, -1e30)
                mask[:, r * 128:(r + 1) * 128] = blk.T
            else:
                mask[:, r * 128:(r + 1) * 128] = -1e30
        m = {
            "x_shard": np.ascontiguousarray(xtiles[core::8].reshape(TL, D)),
            "x_full": x2d,
            "mask8": mask.astype(ml_dtypes.bfloat16),
            "ident_in": ident,
        }
        m.update(wts)
        in_maps.append(m)

    res = run_bass_kernel_spmd(nc, in_maps, core_ids=list(range(NCORE)))
    global LAST_RES
    LAST_RES = res

    out = np.empty((NT, 128, D), dtype=np.float32)
    for core in range(NCORE):
        out[core::8] = res.results[core]["out"].reshape(ML, 128, D)
    return out.reshape(1, T, D)
